# version 06
# speedup vs baseline: 1.4535x; 1.0002x over previous
import numpy as np
import ml_dtypes
import concourse.bass as bass
import concourse.mybir as mybir
from concourse.bass_utils import run_bass_kernel_spmd

F32 = mybir.dt.float32
BF16 = mybir.dt.bfloat16
U8 = mybir.dt.uint8
AF = mybir.ActivationFunctionType
ALU = mybir.AluOpType
AX = mybir.AxisListType

B, T, D = 2, 8192, 1024
TC = 2048
NH, HD = 16, 64
DC, DR = 128, 32
FF, NE, RK = 4096, 8, 16
SCALE_A = float(np.sqrt(HD + DR))
SCALE_B = HD ** (-0.5)
EPS_LN = 1e-5
EPS_RMS = 1e-6
SCALING = 2.0
NCORES = 8
ARENA_BYTES = 212600


class Em:
    ENGS = ("t", "v", "s", "p", "q")
    DSEMS = ("pq", "sq", "ld", "lx", "e0", "e1", "g0", "g1", "h0",
             "h1", "wd", "qa", "ga", "cs", "db")

    def __init__(self):
        self.streams = {e: [] for e in self.ENGS}
        self.counts = {e: 0 for e in self.ENGS}
        self.dma_counts = {s: 0 for s in self.DSEMS}
        self.watermark = {e: {} for e in self.ENGS}

    careful = False

    def _op(self, eng, fn, mark):
        if self.careful and eng == "v":
            mark = True
        if mark:
            self.counts[eng] += 1
            self.streams[eng].append(("op", fn, eng, 1))
        else:
            self.streams[eng].append(("op", fn, None, 0))
        if self.careful and eng == "v":
            self.wait("v", "v", self.counts["v"])
        return self.counts[eng]

    def t(self, fn, mark=False):
        return self._op("t", fn, mark)

    def v(self, fn, mark=False):
        return self._op("v", fn, mark)

    def s(self, fn, mark=False):
        return self._op("s", fn, mark)

    def p(self, fn, mark=False):
        return self._op("p", fn, mark)

    def q(self, fn, mark=False):
        return self._op("q", fn, mark)

    def collective(self, fn, cc_name):
        self.streams["p"].append(("op", fn, cc_name, None))

    def dma(self, q, out, in_, sem=None, **kw):
        eng = "p" if q == "pq" else "q"
        sem = sem or q
        self.dma_counts[sem] += 16
        val = self.dma_counts[sem]

        def fn(e, out=out, in_=in_, kw=kw):
            return e.dma_start(out, in_, **kw)
        self.streams[eng].append(("op", fn, sem, 16))
        return val

    def wait(self, eng, src, val):
        if val is None or val <= 0:
            return
        wm = self.watermark[eng]
        if wm.get(src, 0) >= val:
            return
        wm[src] = val
        self.streams[eng].append(("wait", src, val))

    def waits(self, eng, marks):
        for src, val in marks.items():
            self.wait(eng, src, val)

    def snapshot(self):
        d = dict(self.counts)
        d.update(self.dma_counts)
        return d

    def full_barrier(self):
        snap = self.snapshot()
        for e in self.ENGS:
            self.waits(e, snap)

    def replay(self, engine_objs, sems):
        for ename, stream in self.streams.items():
            eobj = engine_objs[ename]
            for item in stream:
                if item[0] == "op":
                    _, fn, sem_name, inc = item
                    ins = fn(eobj)
                    if sem_name is not None:
                        if inc is None:
                            ins.then_inc(sems[sem_name])
                        else:
                            ins.then_inc(sems[sem_name], inc)
                else:
                    _, src, val = item
                    eobj.wait_ge(sems[src], val)


class Arena:
    def __init__(self, ap_u8, nbytes):
        self.base = ap_u8
        self.nbytes = nbytes
        self.off = 0
        self.peak = 0

    def alloc(self, cols, dt, align=64):
        size = {F32: 4, BF16: 2, U8: 1}[dt]
        self.off = (self.off + align - 1) // align * align
        start = self.off
        self.off += cols * size
        self.peak = max(self.peak, self.off)
        assert self.off <= self.nbytes, \
            f"arena overflow: {self.off} > {self.nbytes}"
        return self.base[:, start:start + cols * size].bitcast(dt)

    def mark(self):
        return self.off

    def reset(self, m):
        self.off = m


class Builder:

    def __init__(self, phases=("attn_a", "ffn1", "attn_b", "ffn2"),
                 debug_dump=False, ser=0, arena_fill=False):
        self.ser = ser
        self.arena_fill = arena_fill
        self.debug_dump = debug_dump
        self.dbg_map = {}
        self.dbg_off = 0
        self.phases = phases
        nc = bass.Bass("TRN2", target_bir_lowering=False, debug=False)
        self.nc = nc
        self.em = Em()
        self.d = {}
        self._declare_dram()
        self._ctxs = []
        arena_t = nc.sbuf_tensor("arena", [128, ARENA_BYTES], U8)
        self._ctxs.append(arena_t)
        self._arena_t = arena_t

    def _declare_dram(self):
        nc = self.nc
        d = self.d

        def din(name, shape, dt=F32):
            d[name] = nc.dram_tensor(name, list(shape), dt,
                                     kind="ExternalInput").ap()

        def dscr(name, shape, dt):
            d[name] = nc.dram_tensor(name, list(shape), dt)

        din("x_fm", (128, 8 * TC))
        d["dbg"] = nc.dram_tensor("dbg", [128, 40960], F32,
                                  kind="ExternalOutput").ap()
        d["y_tok"] = nc.dram_tensor("y_tok", [TC, D], F32,
                                    kind="ExternalOutput").ap()
        din("identf", (128, 128))
        din("selE", (8, 128))
        din("selO", (8, 128))
        din("bsel", (8, 8 * 128))
        din("w1p", (16, 8, 128, 256))
        din("w3p", (16, 8, 128, 256))
        din("w2t", (8, 32, 128, 128))
        din("gateT", (8, 128, 8))
        din("a1T", (8, 128, 128))
        din("a3T", (8, 128, 128))
        din("a2T", (32, 128, 128))
        din("b1T", (128, FF))
        din("b3T", (128, FF))
        din("b2SE", (128, D))
        din("b2SO", (128, D))
        din("normw", (128, 8))
        din("wkvd", (8, 128, 128))
        din("wqTt", (16, 8, 64, 128))
        din("wkupT", (16, 64, 128))
        din("wvup", (128, 1024))
        din("wqrope", (8, 128, 32))
        din("wkrope", (128, 32))
        din("woT", (8, 8, 128, 128))
        din("cosAq", (16, TC))
        din("sinAq", (16, TC))
        din("cosAk", (16, TC))
        din("sinAk", (16, TC))
        din("maskA", (128, 16 * 512))
        din("wqh", (16, 8, 128, 64))
        din("wkh", (16, 8, 128, 64))
        din("wvh", (16, 8, 128, 64))
        din("woutT", (8, 8, 128, 128))
        din("bout", (128, 8))
        din("cosBq", (32, TC))
        din("sinBq", (32, TC))
        din("cosBk", (32, TC))
        din("sinBk", (32, TC))
        dscr("cc1_in", [161, TC], BF16)
        dscr("cc1_out", [644, TC], BF16)
        dscr("cc2_in", [2080, TC], BF16)
        dscr("cc2_out", [8320, TC], BF16)
        dscr("w1bf", [16, 8, 128, 256], BF16)
        dscr("w3bf", [16, 8, 128, 256], BF16)
        dscr("w2bf", [8, 32, 128, 128], BF16)
        dscr("chd", [16, 4, 128, 256], BF16)

    def build(self):
        nc = self.nc
        import contextlib
        stack = contextlib.ExitStack()
        arena_h = stack.enter_context(self._arena_t)
        pb_h = [stack.enter_context(nc.psum_tensor(f"pb{i}", [128, 512], F32))
                for i in range(8)]
        sem_names = ["t", "v", "s", "p", "q", "cc1", "cc2"] + list(Em.DSEMS)
        sems = {n: stack.enter_context(nc.semaphore(f"sem_{n}"))
                for n in sem_names}
        block = stack.enter_context(nc.Block())

        self.ar = Arena(arena_h.ap() if hasattr(arena_h, "ap") else
                        arena_h[:], ARENA_BYTES)
        self.pb = [p[:] for p in pb_h]
        self.pbb = [p[:].bitcast(BF16) for p in pb_h]

        self._alloc_persistent()
        self._emit_all()

        em = self.em
        streams_done = em

        @block.tensor
        def _(t):
            em_replay_one(em, "t", t, sems)

        @block.vector
        def _(v):
            em_replay_one(em, "v", v, sems)

        @block.scalar
        def _(s):
            em_replay_one(em, "s", s, sems)

        @block.gpsimd
        def _(g):
            em_replay_one(em, "p", g, sems)

        @block.sync
        def _(q):
            em_replay_one(em, "q", q, sems)

        stack.close()
        return nc

    def _alloc_persistent(self):
        a = self.ar.alloc
        self.X = a(8 * TC, F32)
        self.ones1b = a(128, BF16)
        self.ones1f = a(128, F32)
        self.onescb = a(1, BF16)
        self.epsc = a(4, F32)
        self.identb = a(128, BF16)
        self.identf = a(128, F32)
        self.persist_mark = self.ar.mark()

    def dbg(self, name, ap, cols, prod_marks):
        if not self.debug_dump:
            return
        em = self.em
        for src_, val in prod_marks.items():
            em.wait("p", src_, val)
        npart = ap.shape[0]
        dst = self.d["dbg"][0:npart, self.dbg_off:self.dbg_off + cols]
        dv = em.dma("pq", dst, ap, sem="db")
        em.wait("v", "db", dv)
        em.wait("s", "db", dv)
        em.wait("t", "db", dv)
        self.dbg_map[name] = (self.dbg_off, cols)
        self.dbg_off += cols
        assert self.dbg_off <= 40960

    def mm(self, out, lhsT, rhs, start, stop, mark=False, tp=None):
        def fn(t, out=out, lhsT=lhsT, rhs=rhs, start=start, stop=stop, tp=tp):
            if tp is None:
                return t.matmul(out, lhsT, rhs, start=start, stop=stop)
            return t.matmul(out, lhsT, rhs, start=start, stop=stop,
                            tile_position=tp)
        return self.em.t(fn, mark=mark)

    def tr(self, out, in_, mark=False, fp32=False):
        k = in_.shape[0]
        ident = self.identf if fp32 else self.identb
        ident = ident[0:k, 0:k]
        return self.em.t(lambda t, out=out, in_=in_, i=ident:
                         t.transpose(out, in_, i), mark=mark)

    def vtt(self, out, a, b, op, mark=False):
        return self.em.v(lambda v, out=out, a=a, b=b, op=op:
                         v.tensor_tensor(out, a, b, op=op), mark=mark)

    def vstt(self, out, in0, scalar, in1, op0, op1, mark=False):
        return self.em.v(
            lambda v, out=out, in0=in0, s=scalar, in1=in1, o0=op0, o1=op1:
            v.scalar_tensor_tensor(out, in0, s, in1, op0=o0, op1=o1),
            mark=mark)

    def vcopy(self, out, in_, mark=False):
        return self.em.v(lambda v, out=out, in_=in_: v.tensor_copy(out, in_),
                         mark=mark)

    def vtsap(self, out, in_, sap, op, mark=False):
        return self.em.v(lambda v, out=out, in_=in_, s1=sap, op=op:
                         v.tensor_scalar(out, in_, s1, None, op0=op),
                         mark=mark)

    def vts(self, out, in_, s1, op, mark=False):
        return self.em.v(lambda v, out=out, in_=in_, s1=s1, op=op:
                         v.tensor_scalar(out, in_, s1, None, op0=op),
                         mark=mark)

    def vrecip(self, out, in_, mark=False):
        m = self.em.v(lambda v, out=out, in_=in_: v.reciprocal(out, in_),
                      mark=True)
        self.em.wait("v", "v", m)
        return m

    def tree8(self, out_col, grp, scratch, op, mark=False):
        self.vtt(scratch[:, 0:4], grp[:, 0:4], grp[:, 4:8], op)
        self.vtt(scratch[:, 4:6], scratch[:, 0:2], scratch[:, 2:4], op)
        return self.vtt(out_col, scratch[:, 4:5], scratch[:, 5:6], op,
                        mark=mark)

    def vred(self, out, in_, op, mark=False):
        m = self.em.v(lambda v, out=out, in_=in_, op=op:
                      v.tensor_reduce(out, in_, axis=AX.X, op=op),
                      mark=True)
        self.em.wait("v", "v", m)
        return m

    def vmemset(self, ap, val, mark=False):
        return self.em.v(lambda v, ap=ap, val=val: v.memset(ap, val),
                         mark=mark)

    def sact(self, out, in_, func, bias=None, scale=None, mark=False):
        def fn(s, out=out, in_=in_, func=func, bias=bias, scale=scale):
            kw = {}
            if bias is not None:
                kw["bias"] = bias
            if scale is not None:
                kw["scale"] = scale
            return s.activation(out, in_, func, **kw)
        return self.em.s(fn, mark=mark)

    def _emit_all(self):
        em = self.em
        if self.arena_fill:
            if self.arena_fill is True:
                lo, hi = self.persist_mark, ARENA_BYTES
            else:
                lo, hi = self.arena_fill
            base = (lo + 63) // 64 * 64
            nf = (hi - base) // 4
            reg = self.ar.base[:, base:base + nf * 4].bitcast(F32)
            mf = self.vmemset(reg, 3.0e4, mark=True)
            for e in ("t", "s", "p", "q"):
                em.wait(e, "v", mf)
        em.dma("sq", self.X, self.d["x_fm"], sem="lx")
        em.dma("pq", self.identb, self.d["identf"], sem="lx")
        self.ld_x = em.dma("sq", self.identf, self.d["identf"], sem="lx")
        self.vmemset(self.ones1b[0:1, :], 1.0)
        self.vmemset(self.onescb, 1.0)
        self.vmemset(self.epsc[:, 0:1], EPS_RMS)
        self.vmemset(self.epsc[:, 1:2], EPS_LN)
        self.vmemset(self.epsc[:, 2:3], -1e30)

        self.m_ones = self.vmemset(self.ones1f[0:1, :], 1.0, mark=True)
        em.wait("t", "v", self.m_ones)
        em.wait("t", "lx", self.ld_x)

        self.prep_end = self.persist_mark
        if "ffn1" in self.phases or "ffn2" in self.phases:
            self._ffn_prep()
            em.full_barrier()
            self.prep_end = self.persist_mark
        if "attn_a" in self.phases:
            self._attn_a()
            self.em.full_barrier()
        if "ffn1" in self.phases:
            self._ffn_call()
            self.em.full_barrier()
        if "attn_b" in self.phases:
            self._attn_b()
            self.em.full_barrier()
        if "ffn2" in self.phases:
            self._ffn_call()
            self.em.full_barrier()
        self._writeback()

    def _writeback(self):
        em = self.em
        ar = self.ar
        ar.reset(self.persist_mark)
        otok = [ar.alloc(1024, F32) for _ in range(2)]
        em.full_barrier()

        prev_store = [0, 0]
        last = 0
        for tt in range(16):
            sl = tt % 2
            for dt in range(8):
                bank = self.pb[dt // 4 + 2 * sl]
                xsl = self.X[:, dt * TC + tt * 128: dt * TC + (tt + 1) * 128]
                self.tr(bank[:, (dt % 4) * 128:(dt % 4 + 1) * 128], xsl,
                        mark=(dt == 7), fp32=True)
            mt = em.counts["t"]
            em.wait("v", "t", mt)
            em.wait("v", f"g{sl}", prev_store[sl])
            mv = self.vcopy(otok[sl][:, 0:512], self.pb[2 * sl][:, 0:512])
            mv = self.vcopy(otok[sl][:, 512:1024],
                            self.pb[2 * sl + 1][:, 0:512], mark=True)
            em.wait("q", "v", mv)
            last = em.dma("sq", self.d["y_tok"][tt * 128:(tt + 1) * 128, :],
                          otok[sl], sem=f"g{sl}")
            prev_store[sl] = last
            em.wait("t", "v", mv)
        em.wait("q", "g0", em.dma_counts["g0"])
        em.wait("q", "g1", em.dma_counts["g1"])

    def _ffn_prep(self):
        em = self.em
        d = self.d
        ar = self.ar
        ar.reset(self.persist_mark)
        stage = [ar.alloc(4096, BF16) for _ in range(2)]
        nwp = ar.alloc(8, F32)
        l_nwp = em.dma("pq", nwp, d["normw"], sem="ld")
        em.wait("v", "ld", l_nwp)
        done_store = [0, 0]
        jobs = []
        for fp in range(16):
            jobs.append((d["w1p"][fp].transpose([1, 0, 2]),
                         d["w1bf"].ap()[fp].transpose([1, 0, 2]), 2048, 8))
            jobs.append((d["w3p"][fp].transpose([1, 0, 2]),
                         d["w3bf"].ap()[fp].transpose([1, 0, 2]), 2048, 8))
        for dt in range(8):
            jobs.append((d["w2t"][dt].transpose([1, 0, 2]),
                         d["w2bf"].ap()[dt].transpose([1, 0, 2]), 4096, 32))
        for i, (srcap, dst, cols, na) in enumerate(jobs):
            sl = i % 2
            em.wait("p", f"g{sl}", done_store[sl])
            st3 = stage[sl][:, 0:cols].rearrange("p (a c) -> p a c", a=na)
            ldv = em.dma("pq", st3, srcap, sem=f"e{sl}")
            if na == 8:
                em.wait("v", f"e{sl}", ldv)
                em.wait("v", f"g{sl}", done_store[sl])
                cpa = cols // 8
                for a_ in range(8):
                    mnw = self.vtsap(
                        stage[sl][:, a_ * cpa:(a_ + 1) * cpa],
                        stage[sl][:, a_ * cpa:(a_ + 1) * cpa],
                        nwp[:, a_:a_ + 1], ALU.mult, mark=(a_ == 7))
                em.wait("q", "v", mnw)
            else:
                em.wait("q", f"e{sl}", ldv)
            done_store[sl] = em.dma("sq", dst, st3, sem=f"g{sl}")
        self.prep_w_done = (em.dma_counts["g0"], em.dma_counts["g1"])

        bpos = ar.alloc(FF, BF16)
        bneg = ar.alloc(FF, BF16)
        chA = ar.alloc(FF, BF16)
        chB = ar.alloc(FF, BF16)
        braw = stage[0]
        chain_stores = []
        prev_sq = 0
        for mat, bT_in in enumerate([d["b1T"], d["b3T"]]):
            em.wait("p", "g0", done_store[0])
            em.wait("p", "sq", prev_sq)
            lb = em.dma("pq", braw[:, 0:FF], bT_in, sem="e0")
            em.wait("v", "e0", lb)
            em.wait("v", "sq", prev_sq)
            self.vts(bpos, braw[:, 0:FF], SCALING, ALU.mult)
            mneg = self.vts(bneg, braw[:, 0:FF], -SCALING, ALU.mult,
                            mark=True)
            z1 = self.vmemset(chA, 0.0)
            z2 = self.vmemset(chB, 0.0, mark=True)
            em.wait("q", "v", z2)
            placement = [(chA, 0, 0), (chA, 32, 3), (chA, 64, 5),
                         (chA, 96, 7), (chB, 0, 1), (chB, 32, 4),
                         (chB, 64, 6), (chB, 96, 2)]
            stv = 0
            for (ch, base, e) in placement:
                if e == 0:
                    stv = em.dma("sq", ch[0:16, :], bpos[0:16, :])
                else:
                    stv = em.dma("sq", ch[base:base + 16, :],
                                 bneg[16 * (e - 1):16 * e, :])
                    stv = em.dma("sq", ch[base + 16:base + 32, :],
                                 bpos[16 * e:16 * (e + 1), :])
            em.wait("q", "sq", stv)
            chd = d["chd"].ap()
            sA = em.dma("sq", chd[:, 2 * mat + 0].transpose([1, 0, 2]),
                        chA.rearrange("p (f c) -> p f c", f=16))
            sB = em.dma("sq", chd[:, 2 * mat + 1].transpose([1, 0, 2]),
                        chB.rearrange("p (f c) -> p f c", f=16))
            chain_stores.append(sB)
            prev_sq = sB
        self.prep_chain_done = chain_stores[-1]
        self.prep_end = ar.mark()

    def _ffn_call(self):
        em = self.em
        d = self.d
        ar = self.ar
        pb = self.pb
        pbb = self.pbb
        ar.reset(self.persist_mark)
        em.full_barrier()
        a = ar.alloc
        NCH, CH = 4, 512

        nxT = a(8 * CH, BF16)
        cact = a(32 * CH, BF16)
        chbuf = [a(4 * 256, BF16) for _ in range(2)]
        w13 = [a(2 * 8 * 256, BF16) for _ in range(2)]
        w2b = [a(32 * 128, BF16) for _ in range(2)]
        a2T = a(32 * 128, BF16)
        a1T, a3T = a(8 * 128, BF16), a(8 * 128, BF16)
        b2E, b2O = a(D, BF16), a(D, BF16)
        gateT = a(8 * 8, F32)
        xwf = a(CH, F32)
        rstdt = a(4, F32)
        mtok = a(4, F32)
        urow = a(8, F32)
        ubc = a(8, F32)
        nwb = a(8, BF16)
        normw = a(8, F32)
        selE, selO = a(128, BF16), a(128, BF16)
        bsel = a(8 * 128, BF16)
        r1A, r1W = a(CH, BF16), a(CH, BF16)
        r3A, r3W = a(CH, BF16), a(CH, BF16)
        xb = [a(CH, BF16), a(CH, BF16)]
        xsq = [a(CH, BF16), a(CH, BF16)]
        gtok = a(CH, F32)
        g32, m1, gm, m2 = a(32, F32), a(4, F32), a(32, F32), a(4, F32)
        dlt, eex, den = a(4, F32), a(4, F32), a(4, F32)
        w1r, w2r = a(4, F32), a(4, F32)
        is1, is2, wAx, wgf = a(32, F32), a(32, F32), a(32, F32), a(32, F32)
        wgt = a(32, BF16)
        wsum, wnr = a(4, F32), a(4, F32)
        wfT = a(CH, BF16)
        wsf = a(CH, BF16)
        bw = a(8 * CH, BF16)
        wslE, wslO = a(CH, BF16), a(CH, BF16)
        wr2E, wr2O = a(CH, BF16), a(CH, BF16)
        rows = a(5 * CH, F32)
        scr = a(CH, F32)
        zrow = a(8, BF16)
        sactb = [a(CH, BF16) for _ in range(2)]
        pactb = [a(CH, BF16) for _ in range(2)]
        wactb = [a(CH, BF16) for _ in range(2)]

        em.wait("q", "sq", self.prep_chain_done)
        em.dma("pq", a1T.rearrange("p (a c) -> p a c", a=8),
               d["a1T"].transpose([1, 0, 2]), sem="ld")
        em.dma("pq", a3T.rearrange("p (a c) -> p a c", a=8),
               d["a3T"].transpose([1, 0, 2]), sem="ld")
        em.dma("pq", a2T.rearrange("p (a c) -> p a c", a=32),
               d["a2T"].transpose([1, 0, 2]), sem="ld")
        em.dma("pq", b2E, d["b2SE"], sem="ld")
        em.dma("pq", b2O, d["b2SO"], sem="ld")
        em.dma("sq", gateT.rearrange("p (a c) -> p a c", a=8),
               d["gateT"].transpose([1, 0, 2]), sem="ld")
        em.dma("pq", normw, d["normw"], sem="ld")
        em.dma("pq", selE[0:8, 0:128], d["selE"], sem="ld")
        em.dma("pq", selO[0:8, 0:128], d["selO"], sem="ld")
        l_all = em.dma("pq", bsel[0:8, :], d["bsel"], sem="ld")
        l_a1 = l_a3 = l_a2 = l_b2e = l_gate = l_nw = l_sE = l_bsel = l_all

        dma_w13 = [0, 0]
        w13_free_t = [0, 0]
        dma_w2 = [0, 0]
        w2_free_v = [0, 0]
        silu_done = [0, 0]
        v1_done = [0, 0]
        util_free_v = 0
        rw_dma = 0
        unit = 0

        def prefetch_w13(tc, fp):
            slot = fp % 2
            em.wait("q", "t", w13_free_t[slot])
            v1 = em.dma("sq",
                        w13[slot][:, 0:8 * 256].rearrange(
                            "p (a c) -> p a c", a=8),
                        d["w1bf"].ap()[fp].transpose([1, 0, 2]),
                        sem=f"e{slot}")
            em.dma("sq",
                   w13[slot][:, 8 * 256:16 * 256].rearrange(
                       "p (a c) -> p a c", a=8),
                   d["w3bf"].ap()[fp].transpose([1, 0, 2]), sem=f"e{slot}")
            v3 = em.dma("sq",
                        chbuf[slot].rearrange("p (a c) -> p a c", a=4),
                        d["chd"].ap()[fp].transpose([1, 0, 2]),
                        sem=f"e{slot}")
            dma_w13[slot] = v3

        def prefetch_w2(tc, dt):
            slot = dt % 2
            em.wait("q", "v", w2_free_v[slot])
            dma_w2[slot] = em.dma(
                "sq", w2b[slot].rearrange("p (a c) -> p a c", a=32),
                d["w2bf"].ap()[dt].transpose([1, 0, 2]), sem=f"g{slot}")

        em.wait("q", "g0", self.prep_w_done[0])
        em.wait("q", "g1", self.prep_w_done[1])
        prefetch_w13(0, 0)
        prefetch_w13(0, 1)

        em.wait("s", "lx", self.ld_x)
        em.wait("v", "lx", self.ld_x)
        em.wait("t", "lx", self.ld_x)
        em.wait("s", "ld", em.dma_counts["ld"])
        em.wait("v", "ld", em.dma_counts["ld"])
        em.wait("t", "ld", em.dma_counts["ld"])
        mz = self.vmemset(zrow[0:1, :], 0.0, mark=True)
        em.wait("t", "v", mz)
        for kt in range(8):
            self.vtsap(a1T[:, kt * 128:(kt + 1) * 128],
                       a1T[:, kt * 128:(kt + 1) * 128],
                       normw[:, kt:kt + 1], ALU.mult)
            mfold = self.vtsap(a3T[:, kt * 128:(kt + 1) * 128],
                               a3T[:, kt * 128:(kt + 1) * 128],
                               normw[:, kt:kt + 1], ALU.mult, mark=(kt == 7))
        em.wait("t", "v", mfold)
        for kt in range(8):
            mu_ = self.mm(pb[6][0:1, 0:8], normw[:, kt:kt + 1],
                          gateT[:, kt * 8:(kt + 1) * 8], start=(kt == 0),
                          stop=(kt == 7), mark=(kt == 7))
        em.wait("v", "t", mu_)
        muv = self.vcopy(urow[0:1, 0:8], pb[6][0:1, 0:8], mark=True)
        em.wait("t", "v", muv)
        mub = self.mm(pb[7][:, 0:8], self.ones1f[0:1, :], urow[0:1, 0:8],
                      True, True, mark=True)
        em.wait("v", "t", mub)
        mubc = self.vcopy(ubc, pb[7][:, 0:8], mark=True)
        em.wait("t", "v", mubc)
        lnbufs = (xb, xsq, rows, scr)
        for tc in range(NCH):
            t0 = tc * CH
            X = self.X
            em.wait("t", "v", util_free_v)
            m_nx = self._attn_ln(t0, CH, nxT, CH, 0, lnbufs)
            util_free_v = m_nx
            if self.ser & 1:
                em.full_barrier()
            if tc == 1:
                self.dbg("nx_c1", nxT, 8 * CH, {"v": m_nx})
                self.dbg("srow_c1", rows[0:1, 3 * CH:4 * CH], CH, {"v": m_nx})

            em.wait("t", "ld", l_gate)
            em.wait("t", "v", m_nx)
            for kt in range(8):
                xsl = X[:, kt * TC + t0: kt * TC + t0 + CH]
                vw = self.vtsap(xwf, xsl, normw[:, kt:kt + 1], ALU.mult,
                                mark=True)
                em.wait("t", "v", vw)
                self.mm(pb[6][0:8, 0:CH], gateT[:, kt * 8:(kt + 1) * 8],
                        xwf, start=(kt == 0), stop=(kt == 7), mark=True)
                if kt < 7:
                    em.wait("v", "t", em.counts["t"])
            mg = em.counts["t"]
            em.wait("v", "t", mg)
            gv = self.vcopy(gtok[0:8, 0:CH], pb[6][0:8, 0:CH], mark=True)
            em.wait("t", "v", gv)
            for qq in range(4):
                self.tr(pb[7][:, qq * 8:(qq + 1) * 8],
                        gtok[0:8, qq * 128:(qq + 1) * 128], mark=(qq == 3),
                        fp32=True)
            mtr = em.counts["t"]
            em.wait("v", "t", mtr)
            for qq in range(4):
                self.tr(pb[6][:, 40 + qq:41 + qq],
                        rows[0:1, 3 * CH + qq * 128:3 * CH + (qq + 1) * 128],
                        fp32=True)
                self.tr(pb[6][:, 44 + qq:45 + qq],
                        rows[0:1, qq * 128:(qq + 1) * 128], mark=(qq == 3),
                        fp32=True)
            mrs = em.counts["t"]
            em.wait("v", "t", mrs)
            em.careful = True
            self.vcopy(rstdt, pb[6][:, 40:44])
            self.vcopy(mtok, pb[6][:, 44:48])
            self.vcopy(g32, pb[7][:, 0:32], mark=True)
            for aa in range(4):
                self.vtsap(wAx[:, 8 * aa:8 * aa + 8], ubc[:, 0:8],
                           mtok[:, aa:aa + 1], ALU.mult)
            self.vtt(g32, g32, wAx, ALU.subtract)
            for aa in range(4):
                self.vtsap(g32[:, 8 * aa:8 * aa + 8], g32[:, 8 * aa:8 * aa + 8],
                           rstdt[:, aa:aa + 1], ALU.mult)
            for aa in range(4):
                self.tree8(m1[:, aa:aa + 1], g32[:, 8 * aa:8 * aa + 8],
                           gm[:, 8 * aa:8 * aa + 8], ALU.max)
            for aa in range(4):
                self.vtsap(is1[:, 8 * aa:8 * aa + 8],
                           g32[:, 8 * aa:8 * aa + 8], m1[:, aa:aa + 1],
                           ALU.is_ge)
            self.vstt(gm, is1, self.epsc[:, 2:3], g32, ALU.mult, ALU.add)
            for aa in range(4):
                self.tree8(m2[:, aa:aa + 1], gm[:, 8 * aa:8 * aa + 8],
                           wAx[:, 8 * aa:8 * aa + 8], ALU.max)
            for aa in range(4):
                self.vtsap(is2[:, 8 * aa:8 * aa + 8],
                           gm[:, 8 * aa:8 * aa + 8], m2[:, aa:aa + 1],
                           ALU.is_ge)
            vd0 = self.vtt(dlt, m2, m1, ALU.subtract, mark=True)
            vd = vd0
            em.wait("s", "v", vd)
            self.sact(w1r, dlt, AF.Sigmoid, scale=-1.0)
            ee = self.sact(w2r, dlt, AF.Sigmoid, mark=True)
            em.wait("v", "s", ee)
            for aa in range(4):
                self.vtsap(wAx[:, 8 * aa:8 * aa + 8],
                           is1[:, 8 * aa:8 * aa + 8], w1r[:, aa:aa + 1],
                           ALU.mult)
                self.vtsap(wgf[:, 8 * aa:8 * aa + 8],
                           is2[:, 8 * aa:8 * aa + 8], w2r[:, aa:aa + 1],
                           ALU.mult)
            self.vtt(wgf, wgf, wAx, ALU.add)
            for aa in range(4):
                self.tree8(wsum[:, aa:aa + 1], wgf[:, 8 * aa:8 * aa + 8],
                           is1[:, 8 * aa:8 * aa + 8], ALU.add,
                           mark=(aa == 3))
            em.wait("s", "v", em.counts["v"])
            self.sact(wnr, wsum, AF.Ln)
            mwnr = self.sact(wnr, wnr, AF.Exp, scale=-1.0, mark=True)
            em.wait("v", "s", mwnr)
            wg_m = 0
            for aa in range(4):
                wg_m = self.vtsap(wgt[:, 8 * aa:8 * aa + 8],
                                  wgf[:, 8 * aa:8 * aa + 8],
                                  wnr[:, aa:aa + 1], ALU.mult,
                                  mark=(aa == 3))
            em.careful = False
            if tc == 1:
                self.dbg("nx_c1", nxT, 8 * CH, {"v": m_nx})
                self.dbg("srow_c1", rows[0:1, 3 * CH:4 * CH], CH, {"v": m_nx})
                self.dbg("g32_0", g32, 32, {"v": wg_m})
            em.wait("t", "v", wg_m)
            if tc == 1:
                self.dbg("wgt_c1", wgt, 32, {"v": wg_m})
                self.dbg("g32_c1", g32, 32, {"v": wg_m})
            for qq in range(4):
                self.tr(pbb[7][0:8, 64 + qq * 128: 64 + (qq + 1) * 128],
                        wgt[:, qq * 8:(qq + 1) * 8], mark=(qq == 3))
            mtw = em.counts["t"]
            em.wait("v", "t", mtw)
            wfv = self.vcopy(wfT[0:8, 0:CH], pbb[7][0:8, 64:64 + CH],
                             mark=True)
            wsv = self.vts(wsf[0:8, 0:CH], wfT[0:8, 0:CH], SCALING, ALU.mult,
                           mark=True)
            em.wait("t", "v", wfv)
            em.wait("t", "ld", l_bsel)
            bw_ev = {}
            for e in range(8):
                bank = pb[6] if e % 2 == 0 else pb[7]
                if e >= 2:
                    em.wait("t", "v", bw_ev[e - 2])
                self.mm(bank[:, 0:CH], bsel[0:8, e * 128:(e + 1) * 128],
                        wfT[0:8, 0:CH], True, True, mark=True)
                mk = em.counts["t"]
                em.wait("v", "t", mk)
                bw_ev[e] = self.vcopy(bw[:, e * CH:(e + 1) * CH],
                                      bank[:, 0:CH], mark=True)
            em.wait("t", "ld", l_sE)
            em.wait("t", "v", bw_ev[6])
            self.mm(pb[6][:, 0:CH], selE[0:8, 0:128], wsf[0:8, 0:CH],
                    True, True, mark=True)
            me_ = em.counts["t"]
            em.wait("t", "v", bw_ev[7])
            self.mm(pb[7][:, 0:CH], selO[0:8, 0:128], wsf[0:8, 0:CH],
                    True, True, mark=True)
            mo_ = em.counts["t"]
            em.wait("v", "t", me_)
            self.vcopy(wslE, pb[6][:, 0:CH])
            em.wait("v", "t", mo_)
            mwsl = self.vcopy(wslO, pb[7][:, 0:CH], mark=True)

            if self.ser & 2:
                em.full_barrier()
            em.wait("t", "ld", l_a1)
            em.wait("t", "v", mwsl)
            for kt in range(8):
                self.mm(pb[6][:, 0:CH], a1T[:, kt * 128:(kt + 1) * 128],
                        nxT[:, kt * CH:(kt + 1) * CH], start=(kt == 0),
                        stop=(kt == 7), mark=(kt == 7))
            mr1 = em.counts["t"]
            em.wait("t", "ld", l_a3)
            for kt in range(8):
                self.mm(pb[7][:, 0:CH], a3T[:, kt * 128:(kt + 1) * 128],
                        nxT[:, kt * CH:(kt + 1) * CH], start=(kt == 0),
                        stop=(kt == 7), mark=(kt == 7))
            mr3 = em.counts["t"]
            em.wait("v", "t", mr1)
            e1 = self.vcopy(r1A, pb[6][:, 0:CH], mark=True)
            em.wait("v", "t", mr3)
            e3 = self.vcopy(r3A, pb[7][:, 0:CH], mark=True)
            util_free_v = e3
            em.wait("q", "v", e1)
            em.dma("sq", r1W[32:96, :], r1A[48:112, :], sem="wd")
            em.dma("sq", r1W[96:128, :], r1A[16:48, :], sem="wd")
            em.wait("q", "v", e3)
            em.dma("sq", r3W[32:96, :], r3A[48:112, :], sem="wd")
            rw_dma = em.dma("sq", r3W[96:128, :], r3A[16:48, :], sem="wd")
            if tc == 1:
                self.dbg("nx_c1", nxT, 8 * CH, {"v": m_nx})
                self.dbg("srow_c1", rows[0:1, 3 * CH:4 * CH], CH, {"v": m_nx})
                pass

            def win(rA, rW):
                return {0: (rA, 0), 1: (rA, 0), 2: (rW, 96), 3: (rA, 32),
                        4: (rW, 32), 5: (rA, 64), 6: (rW, 64), 7: (rA, 96)}
            w1map = win(r1A, r1W)
            w3map = win(r3A, r3W)
            lhs1 = {0: (0, 0), 1: (1, 0), 2: (1, 96), 3: (0, 32),
                    4: (1, 32), 5: (0, 64), 6: (1, 64), 7: (0, 96)}
            lhs3 = {0: (2, 0), 1: (3, 0), 2: (3, 96), 3: (2, 32),
                    4: (3, 32), 5: (2, 64), 6: (3, 64), 7: (2, 96)}

            if self.ser & 4:
                em.full_barrier()
            em.wait("t", "ld", l_a2)
            em.wait("t", "wd", rw_dma)
            self.mm(pb[4][:, 0:CH], zrow[0:1, 0:1].broadcast_to([1, 128]),
                    zrow[0:1, 0:1].broadcast_to([1, CH]), True, False)
            self.mm(pb[5][:, 0:CH], zrow[0:1, 0:1].broadcast_to([1, 128]),
                    zrow[0:1, 0:1].broadcast_to([1, CH]), True, False)
            r2_started = set()
            for fp in range(16):
                slot = fp % 2
                for sub in range(2):
                    s_ = unit % 2
                    H1, H3 = pb[2 * s_], pb[2 * s_ + 1]
                    em.wait("t", f"e{slot}", dma_w13[slot])
                    if self.debug_dump and tc == 0 and fp == 0 and sub == 0:
                        pass
                    em.wait("t", "s", silu_done[s_])
                    em.wait("t", "v", v1_done[s_])
                    for half in range(2):
                        for kt in range(8):
                            self.mm(H1[:, half * 256:(half + 1) * 256],
                                    w13[slot][:, (0 * 8 + kt) * 256 +
                                              half * 128:
                                              (0 * 8 + kt) * 256 +
                                              half * 128 + 128],
                                    nxT[:, kt * CH + sub * 256:
                                        kt * CH + sub * 256 + 256],
                                    start=(kt == 0 and half == 0),
                                    stop=(kt == 7))
                    for half in range(2):
                        for kt in range(8):
                            self.mm(H3[:, half * 256:(half + 1) * 256],
                                    w13[slot][:, (1 * 8 + kt) * 256 +
                                              half * 128:
                                              (1 * 8 + kt) * 256 +
                                              half * 128 + 128],
                                    nxT[:, kt * CH + sub * 256:
                                        kt * CH + sub * 256 + 256],
                                    start=(kt == 0 and half == 0),
                                    stop=(kt == 7))
                    for e in range(8):
                        (rT, rb) = w1map[e]
                        (ci, lb) = lhs1[e]
                        tp = (96, 0) if rb == 96 else None
                        if e > 0:
                            em.wait("t", "s", silu_done[s_])
                        for half in range(2):
                            self.mm(H1[:, half * 256:(half + 1) * 256],
                                    chbuf[slot][lb:lb + 32,
                                                ci * 256 + half * 128:
                                                ci * 256 + half * 128 + 128],
                                    rT[rb:rb + 32, sub * 256:sub * 256 + 256],
                                    start=False, stop=(half == 1),
                                    mark=(half == 1), tp=tp)
                        mh1 = em.counts["t"]
                        (rT3, rb3) = w3map[e]
                        (ci3, lb3) = lhs3[e]
                        tp3 = (96, 0) if rb3 == 96 else None
                        if e > 0:
                            em.wait("t", "v", v1_done[s_])
                        for half in range(2):
                            self.mm(H3[:, half * 256:(half + 1) * 256],
                                    chbuf[slot][lb3:lb3 + 32,
                                                ci3 * 256 + half * 128:
                                                ci3 * 256 + half * 128 + 128],
                                    rT3[rb3:rb3 + 32,
                                        sub * 256:sub * 256 + 256],
                                    start=False, stop=(half == 1),
                                    mark=(half == 1), tp=tp3)
                        mh3 = em.counts["t"]
                        em.wait("s", "t", mh1)
                        em.wait("s", "v", v1_done[s_])
                        silu_done[s_] = self.sact(sactb[s_], H1[:, 0:512],
                                                  AF.Silu, mark=True)
                        em.wait("v", "s", silu_done[s_])
                        em.wait("v", "t", mh3)
                        v1_done[s_] = self.vtt(pactb[s_], sactb[s_],
                                               H3[:, 0:512], ALU.mult,
                                               mark=True)
                        if (self.debug_dump and tc == 0 and fp == 0
                                and sub == 0 and e in (0, 7)):
                            pass
                            pass
                        em.wait("t", "v", v1_done[s_])
                        bankR = pb[4] if e % 2 == 0 else pb[5]
                        rowb = 16 * e if e % 2 == 0 else 16 * (e - 1)
                        stt = False
                        for half in range(2):
                            ft = 2 * fp + half
                            self.mm(bankR[rowb:rowb + 16,
                                          sub * 256:sub * 256 + 256],
                                    a2T[:, ft * 128 + 16 * e:
                                        ft * 128 + 16 * e + 16],
                                    pactb[s_][:, half * 256:
                                              (half + 1) * 256],
                                    start=(stt and half == 0),
                                    stop=False,
                                    tp=(0, rowb) if rowb in (0, 32, 64)
                                    else (0, 96))
                        bwe = bw[:, e * CH + sub * 256: e * CH +
                                 sub * 256 + 256]
                        self.vtt(wactb[s_][:, 0:256], pactb[s_][:, 0:256],
                                 bwe, ALU.mult)
                        self.vtt(wactb[s_][:, 256:512], pactb[s_][:, 256:512],
                                 bwe, ALU.mult)
                        co = cact.rearrange("p (f t) -> p f t", f=32)[
                            :, 2 * fp:2 * fp + 2, sub * 256:sub * 256 + 256]
                        if fp == 0:
                            pass
                        wv = wactb[s_].rearrange("p (h t) -> p h t", h=2)
                        if e == 0:
                            self.vcopy(co, wv, mark=False)
                        else:
                            self.vtt(co, wv, co, ALU.add, mark=False)
                        unitlast_v = self.em.counts["v"]
                    self.em.t(lambda t: t.nop(), mark=True)
                    w13_free_t[slot] = em.counts["t"]
                    unit += 1
                if self.ser & 8:
                    em.full_barrier()
                if fp + 2 < 16:
                    prefetch_w13(tc, fp + 2)
                elif tc + 1 < NCH:
                    prefetch_w13(tc + 1, fp + 2 - 16)

            self.em.t(lambda t: t.nop(), mark=True)
            mr2 = em.counts["t"]
            if tc == 1:
                self.dbg("cact_c1", cact[:, 0:4096], 4096,
                         {"v": em.counts["v"], "t": mr2})
                em.wait("v", "pq", self.em.dma_counts["pq"])
            em.wait("v", "t", mr2)
            ve = self.vtt(wr2E, pb[4][:, 0:CH], wslE, ALU.mult, mark=True)
            vo = self.vtt(wr2O, pb[5][:, 0:CH], wslO, ALU.mult, mark=True)
            if tc == 1:
                self.dbg("wr2E_c1", wr2E, CH, {"v": ve})
                self.dbg("wr2O_c1", wr2O, CH, {"v": vo})
                self.dbg("wslE_c1", wslE, CH, {"v": ve})
            if self.ser & 16:
                em.full_barrier()
            down_resid = {}
            prefetch_w2(tc, 0)
            prefetch_w2(tc, 1)
            for dt in range(8):
                slot = dt % 2
                bank = pb[6] if dt % 2 == 0 else pb[7]
                em.wait("t", f"g{slot}", dma_w2[slot])
                em.wait("t", "v", util_free_v)
                if dt >= 2:
                    em.wait("t", "v", down_resid[dt - 2])
                for kt in range(32):
                    self.mm(bank[:, 0:CH],
                            w2b[slot][:, kt * 128:(kt + 1) * 128],
                            cact[:, kt * CH:(kt + 1) * CH],
                            start=(kt == 0), stop=False)
                em.wait("t", "v", ve if dt % 2 == 0 else vo)
                em.wait("t", "ld", l_b2e)
                self.mm(bank[:, 0:CH], b2E[:, dt * 128:(dt + 1) * 128],
                        wr2E, start=False, stop=False)
                md = self.mm(bank[:, 0:CH], b2O[:, dt * 128:(dt + 1) * 128],
                             wr2O, start=False, stop=True, mark=True)
                em.wait("v", "t", md)
                xsl = self.X[:, dt * TC + t0: dt * TC + t0 + CH]
                down_resid[dt] = self.vtt(xsl, xsl, bank[:, 0:CH], ALU.add,
                                          mark=True)
                if tc == 1 and dt in (0, 1):
                    self.dbg(f"Xc1after{dt}", xsl, CH, {"v": down_resid[dt]})
                w2_free_v[slot] = down_resid[dt]
                if dt + 2 < 8:
                    prefetch_w2(tc, dt + 2)
            util_free_v = down_resid[7]
            if self.ser & 32:
                em.full_barrier()
            em.wait("v", "t", em.counts["t"])
            em.wait("t", "v", vo)

    def _attn_ln(self, t0, CH, dst, dst_stride, dst_off, bufs):
        em = self.em
        pb = self.pb
        xb, xsq, rows, scr = bufs
        X = self.X
        em.wait("t", "v", em.counts["v"])
        em.wait("s", "lx", self.ld_x)
        st_m = {}
        for dt in range(8):
            sl = dt % 2
            xsl = X[:, dt * TC + t0: dt * TC + t0 + CH]
            if dt >= 2:
                em.wait("s", "t", st_m[dt - 2])
            self.sact(xb[sl], xsl, AF.Copy)
            m = self.sact(xsq[sl], xsl, AF.Square, mark=True)
            em.wait("t", "s", m)
            self.mm(pb[6][0:1, 0:CH], self.onescb, xb[sl],
                    start=(dt == 0), stop=(dt == 7))
            st_m[dt] = self.mm(pb[7][0:1, 0:CH], self.onescb, xsq[sl],
                               start=(dt == 0), stop=(dt == 7), mark=True)
        mstat = em.counts["t"]
        em.wait("v", "t", mstat)
        em.careful = True
        mean = rows[0:1, 0:CH]
        m2r = rows[0:1, CH:2 * CH]
        varr = rows[0:1, 2 * CH:3 * CH]
        srow = rows[0:1, 3 * CH:4 * CH]
        msr = rows[0:1, 4 * CH:5 * CH]
        self.vts(mean, pb[6][0:1, 0:CH], 1.0 / D, ALU.mult)
        self.vts(m2r, pb[7][0:1, 0:CH], 1.0 / D, ALU.mult)
        self.vtt(varr, mean, mean, ALU.mult)
        self.vtt(varr, m2r, varr, ALU.subtract)
        vv = self.vts(varr, varr, EPS_LN, ALU.add, mark=True)
        em.wait("s", "v", vv)
        self.sact(varr, varr, AF.Ln)
        rs = self.sact(srow, varr, AF.Exp, scale=-0.5, mark=True)
        em.wait("v", "s", rs)
        mm_ = self.vtt(msr, mean, srow, ALU.mult, mark=True)
        em.careful = False
        em.wait("t", "v", mm_)
        self.mm(pb[6][:, 0:CH], self.ones1f[0:1, :], srow, True, True)
        mb = self.mm(pb[7][:, 0:CH], self.ones1f[0:1, :], msr, True, True,
                     mark=True)
        em.wait("v", "t", mb)
        for dt in range(8):
            xsl = X[:, dt * TC + t0: dt * TC + t0 + CH]
            dsl = dst[:, dt * dst_stride + dst_off:
                      dt * dst_stride + dst_off + CH]
            self.vtt(scr, xsl, pb[6][:, 0:CH], ALU.mult)
            self.vtt(dsl, scr, pb[7][:, 0:CH], ALU.subtract, mark=(dt == 7))
        return em.counts["v"]

    def _rope_apply(self, x1ps, x2ps, tab, tcol, o1, o2, CH, H):
        cos = tab[0:H, tcol:tcol + CH]
        sin = tab[32:32 + H, tcol:tcol + CH]
        scr1 = self._rp_scr1[0:H, 0:CH]
        scr2 = self._rp_scr2[0:H, 0:CH]
        self.vtt(scr1, x1ps, cos, ALU.mult)
        self.vtt(scr2, x2ps, sin, ALU.mult)
        self.vtt(o1, scr1, scr2, ALU.subtract)
        self.vtt(scr1, x2ps, cos, ALU.mult)
        self.vtt(scr2, x1ps, sin, ALU.mult)
        return self.vtt(o2, scr1, scr2, ALU.add, mark=True)

    def _attn_a(self):
        em = self.em
        d = self.d
        ar = self.ar
        pb = self.pb
        ar.reset(self.prep_end)
        a = ar.alloc
        CH = 512

        nxT = a(8 * CH, BF16)
        ckv_own = a(TC, BF16)
        kr1 = a(TC, BF16)
        kr2 = a(TC, BF16)
        negrow = a(TC, BF16)
        ckvG = a(4 * TC, BF16)
        kaug = a(4 * TC, BF16)
        vh = a(64 * 65, BF16)
        attn_all = a(8 * CH, BF16)
        maskA = a(16 * CH, BF16)
        tabQ = a(TC, BF16)
        tabK = a(TC, BF16)
        wqsl = [a(8 * 128, BF16) for _ in range(2)]
        wkupsl = [a(128, BF16) for _ in range(2)]
        for _sl in range(2):
            self.vmemset(wqsl[_sl][64:96, :], 0.0)
            self.vmemset(wkupsl[_sl][64:96, :], 0.0)
        wosl = [a(8 * 128, BF16) for _ in range(2)]
        wkvd = a(8 * 128, BF16)
        wvup = a(1024, BF16)
        wkrope = a(32, BF16)
        wqrope = a(8 * 32, BF16)
        wtl = a(8 * 128, BF16)
        qt = a(CH, BF16)
        qaug = a(CH, BF16)
        q1s = a(CH, BF16)
        q2s = a(CH, BF16)
        expb = [a(CH, BF16) for _ in range(2)]
        bcsb = a(CH, BF16)
        linb = a(CH, BF16)
        xb = [a(CH, BF16), a(CH, BF16)]
        xsq = [a(CH, BF16), a(CH, BF16)]
        rows = a(5 * CH, F32)
        scr = a(CH, F32)
        self._rp_scr1 = a(CH, F32)
        self._rp_scr2 = a(CH, F32)
        nrope = rows[0:1, 0:CH]
        b2row = rows[0:1, CH:2 * CH]
        linv = rows[0:1, 2 * CH:3 * CH]
        mkrow = a(8, F32)

        em.dma("pq", wkvd.rearrange("p (a c) -> p a c", a=8),
               d["wkvd"].transpose([1, 0, 2]), sem="ld")
        em.dma("pq", wvup, d["wvup"], sem="ld")
        em.dma("pq", wkrope, d["wkrope"], sem="ld")
        em.dma("pq", wqrope.rearrange("p (a c) -> p a c", a=8),
               d["wqrope"].transpose([1, 0, 2]), sem="ld")
        em.dma("pq", maskA, d["maskA"], sem="ld")
        em.dma("pq", tabQ[0:16, :], d["cosAq"], sem="ld")
        em.dma("pq", tabQ[32:48, :], d["sinAq"], sem="ld")
        em.dma("pq", tabK[0:16, :], d["cosAk"], sem="ld")
        l_all = em.dma("pq", tabK[32:48, :], d["sinAk"], sem="ld")
        l_wkvd = l_wvup = l_wkrope = l_wqrope = l_mask = l_tq2 = l_tk2 \
            = l_all
        mneg = self.vmemset(negrow[0:1, :], -1.0, mark=True)
        self.vmemset(kaug[32:64, :], 0.0)
        self.vmemset(kaug[64:96, :], 0.0)
        self.vmemset(qaug[32:64, 0:CH], 0.0)
        m_zk = self.vmemset(qaug[64:96, 0:CH], 0.0, mark=True)
        em.wait("t", "v", m_zk)
        mvones = self.vmemset(
            vh.rearrange("p (kt x) -> p kt x", kt=64)[:, :, 64:65], 1.0,
            mark=True)
        lnbufs = (xb, xsq, rows, scr)

        for c in range(4):
            t0 = c * CH
            m_nx = self._attn_ln(t0, CH, nxT, CH, 0, lnbufs)
            em.wait("t", "v", m_nx)
            em.wait("t", "ld", l_wkvd)
            for kt in range(8):
                self.mm(pb[0][:, 0:CH], wkvd[:, kt * 128:(kt + 1) * 128],
                        nxT[:, kt * CH:(kt + 1) * CH], start=(kt == 0),
                        stop=(kt == 7), mark=(kt == 7))
            mck = em.counts["t"]
            em.wait("v", "t", mck)
            mce = self.vcopy(ckv_own[:, t0:t0 + CH], pb[0][:, 0:CH],
                             mark=True)
            em.wait("t", "v", mce)
            em.wait("t", "ld", l_wkrope)
            self.mm(pb[1][0:16, 0:CH], wkrope[:, 0:16],
                    ckv_own[:, t0:t0 + CH], True, True)
            mkr = self.mm(pb[2][0:16, 0:CH], wkrope[:, 16:32],
                          ckv_own[:, t0:t0 + CH], True, True, mark=True)
            em.wait("v", "t", mkr)
            em.wait("v", "ld", l_tk2)
            self._rope_apply(pb[1][0:16, 0:CH], pb[2][0:16, 0:CH], tabK, t0,
                             kr1[0:16, t0:t0 + CH], kr2[0:16, t0:t0 + CH],
                             CH, 16)
        mkrope_all = em.counts["v"]
        em.wait("q", "v", mkrope_all)
        em.wait("q", "v", mneg)
        cc1 = d["cc1_in"].ap()
        em.dma("sq", cc1[0:128, :], ckv_own, sem="cs")
        em.dma("sq", cc1[128:144, :], kr1[0:16, :], sem="cs")
        em.dma("sq", cc1[144:160, :], kr2[0:16, :], sem="cs")
        ck_dma = em.dma("sq", cc1[160:161, :], negrow[0:1, :], sem="cs")
        em.wait("p", "cs", ck_dma)

        def ccfn(g):
            return g.collective_compute(
                "AllGather", ALU.bypass,
                replica_groups=[[0, 1, 2, 3], [4, 5, 6, 7]],
                ins=[d["cc1_in"].ap().opt()], outs=[d["cc1_out"].ap().opt()])
        em.collective(ccfn, "cc1")
        em.wait("p", "cc1", 1)
        em.wait("p", "v", m_zk)
        cc1o = d["cc1_out"].ap()
        for r in range(4):
            em.dma("pq", ckvG[:, r * TC:(r + 1) * TC],
                   cc1o[161 * r:161 * r + 128, :], sem="ga")
            l_gath = em.dma("pq", kaug[0:33, r * TC:(r + 1) * TC],
                            cc1o[161 * r + 128:161 * r + 161, :], sem="ga")

        em.wait("s", "ga", l_gath)
        em.wait("t", "ga", l_gath)
        for r in range(4):
            em.wait("t", "v", em.counts["v"])
            mk_m = 0
            for cc_ in range(4):
                sl = cc_ % 2
                if cc_ >= 2:
                    em.wait("s", "t", mk_m)
                self.sact(xsq[sl], ckvG[:, r * TC + cc_ * CH:
                                        r * TC + (cc_ + 1) * CH], AF.Square)
                m1_ = self.sact(xb[sl][0:33, :],
                                kaug[0:33, r * TC + cc_ * CH:
                                     r * TC + (cc_ + 1) * CH],
                                AF.Square, mark=True)
                em.wait("t", "s", m1_)
                self.mm(pb[3][0:1, 0:CH], self.onescb, xsq[sl],
                        start=(cc_ == 0), stop=False)
                mk_m = self.mm(pb[3][0:1, 0:CH], self.onescb[0:33, :],
                               xb[sl][0:33, :], start=False, stop=(cc_ == 3),
                               mark=True)
            em.wait("v", "t", mk_m)
            em.careful = True
            self.vred(rows[0:1, 0:1], pb[3][0:1, 0:CH], ALU.max)
            self.vcopy(mkrow[0:1, r:r + 1], rows[0:1, 0:1], mark=True)
            em.careful = False
        em.careful = True
        self.vred(mkrow[0:1, 4:5], mkrow[0:1, 0:4], ALU.max, mark=True)
        mks = em.counts["v"]
        em.wait("s", "v", mks)
        self.sact(mkrow[0:1, 5:6], mkrow[0:1, 4:5], AF.Ln)
        mkf = self.sact(mkrow[0:1, 6:7], mkrow[0:1, 5:6], AF.Exp, scale=0.5,
                        mark=True)
        em.careful = False

        wq_dma = [0, 0]
        wq_free = [0, 0]
        wo_dma = [0, 0]
        wo_free = [0, 0]

        def prefetch_wq(h):
            slot = h % 2
            em.wait("p", "t", wq_free[slot])
            em.dma("pq", wqsl[slot][0:64, :].rearrange(
                "p (a c) -> p a c", a=8), d["wqTt"][h].transpose([1, 0, 2]),
                sem=f"e{slot}")
            wq_dma[slot] = em.dma("pq", wkupsl[slot][0:64, :],
                                  d["wkupT"][h], sem=f"e{slot}")

        exp_free = [0, 0]
        sc_free = [0, 0]

        for c in range(4):
            t0 = c * CH
            m_nx = self._attn_ln(t0, CH, nxT, CH, 0, lnbufs)
            em.wait("t", "v", m_nx)
            em.wait("t", "ld", l_wqrope)
            for kt in range(8):
                self.mm(pb[4][0:16, 0:CH], wqrope[:, kt * 32:kt * 32 + 16],
                        nxT[:, kt * CH:(kt + 1) * CH], start=(kt == 0),
                        stop=(kt == 7))
                self.mm(pb[5][0:16, 0:CH],
                        wqrope[:, kt * 32 + 16:kt * 32 + 32],
                        nxT[:, kt * CH:(kt + 1) * CH], start=(kt == 0),
                        stop=(kt == 7), mark=(kt == 7))
            mqr = em.counts["t"]
            em.wait("v", "t", mqr)
            em.wait("v", "ld", l_tq2)
            mrope = self._rope_apply(pb[4][0:16, 0:CH], pb[5][0:16, 0:CH],
                                     tabQ, t0, q1s[0:16, 0:CH],
                                     q2s[0:16, 0:CH], CH, 16)
            mq1 = self.vcopy(qaug[0:16, 0:CH], q1s[0:16, 0:CH], mark=True)
            em.wait("q", "v", mq1)
            qa_dma = em.dma("sq", qaug[16:32, 0:CH], q2s[0:16, 0:CH], sem="qa")
            em.wait("s", "v", mq1)
            s1 = self.sact(xsq[0][0:16, 0:CH], q1s[0:16, 0:CH], AF.Square)
            s2 = self.sact(xsq[1][0:16, 0:CH], q2s[0:16, 0:CH], AF.Square,
                           mark=True)
            em.wait("t", "s", s2)
            em.wait("t", "v", mq1)
            self.mm(pb[4][0:1, 0:CH], self.onescb[0:16, :],
                    xsq[0][0:16, 0:CH], True, False)
            mnr = self.mm(pb[4][0:1, 0:CH], self.onescb[0:16, :],
                          xsq[1][0:16, 0:CH], False, True, mark=True)
            em.wait("v", "t", mnr)
            mnrv = self.vcopy(nrope[0:1, 0:CH], pb[4][0:1, 0:CH], mark=True)

            prefetch_wq(0)
            prefetch_wq(1)
            for h in range(NH):
                slot = h % 2
                em.wait("t", f"e{slot}", wq_dma[slot])
                em.wait("t", "v", mnrv)
                for dt in range(8):
                    bank = pb[4] if dt < 4 else pb[5]
                    self.mm(bank[:, (dt % 4) * 128:(dt % 4 + 1) * 128],
                            wqsl[slot][0:96, dt * 128:(dt + 1) * 128],
                            wkupsl[slot][0:96, :], True, True,
                            mark=(dt == 3 or dt == 7))
                mwt = em.counts["t"]
                em.wait("v", "t", mwt)
                self.vcopy(wtl[:, 0:512], pb[4][:, 0:CH])
                mwv = self.vcopy(wtl[:, 512:1024], pb[5][:, 0:CH], mark=True)
                wq_free[slot] = mwt
                em.wait("t", "v", mwv)
                em.wait("t", "s", getattr(self, "_mqs_prev", 0))
                for dt in range(8):
                    self.mm(pb[6][:, 0:CH], wtl[:, dt * 128:(dt + 1) * 128],
                            nxT[:, dt * CH:(dt + 1) * CH],
                            start=(dt == 0), stop=(dt == 7), mark=(dt == 7))
                mqt = em.counts["t"]
                em.wait("s", "t", mqt)
                mqs = self.em.s(lambda s, o=qt, i=pb[6][:, 0:CH]:
                                s.mul(o, i, 1.0 / SCALE_A), mark=True)
                self._mqs_prev = mqs
                sqq = self.sact(xsq[0], qt, AF.Square, mark=True)
                em.wait("t", "s", sqq)
                em.wait("t", "v", getattr(self, "_b2_prev", 0))
                self.mm(pb[7][0:1, 0:CH], self.onescb, xsq[0], True, True,
                        mark=True)
                mbn = em.counts["t"]
                em.wait("v", "t", mbn)
                em.wait("v", "v", mnrv)
                em.careful = True
                self._b2_prev = self.vtt(b2row[0:1, 0:CH], pb[7][0:1, 0:CH],
                                         nrope[0:1, 0:CH], ALU.add,
                                         mark=True)
                em.wait("s", "v", em.counts["v"])
                self.sact(b2row[0:1, 0:CH], b2row[0:1, 0:CH], AF.Ln)
                mbs = self.sact(b2row[0:1, 0:CH], b2row[0:1, 0:CH], AF.Exp,
                                scale=0.5, mark=True)
                em.wait("v", "s", mbs)
                em.wait("v", "s", mkf)
                self.vtsap(b2row[0:1, 0:CH], b2row[0:1, 0:CH],
                           mkrow[0:1, 6:7], ALU.mult)
                self.vts(b2row[0:1, 0:CH], b2row[0:1, 0:CH], 1.02, ALU.mult)
                em.wait("v", "qa", qa_dma)
                mba = self.vcopy(qaug[32:33, 0:CH], b2row[0:1, 0:CH],
                                 mark=True)
                em.careful = False
                em.wait("t", "v", mba)
                em.wait("t", "ga", l_gath)
                em.wait("t", "ld", l_wvup)
                em.wait("t", "v", mvones)
                nkt = 16 * (c + 1)
                for grp in range(nkt // 8):
                    for ii in range(8):
                        idx = grp * 8 + ii
                        r_ = idx % 4
                        m_ = idx // 4
                        self.mm(pb[3][:, ii * 64:(ii + 1) * 64],
                                ckvG[:, r_ * TC + 128 * m_:
                                     r_ * TC + 128 * m_ + 128],
                                wvup[:, h * 64:(h + 1) * 64],
                                start=True, stop=True, mark=(ii == 7))
                    mvb = em.counts["t"]
                    em.wait("v", "t", mvb)
                    mvc = self.em.v(
                        lambda v, grp=grp:
                        v.tensor_copy(
                            vh.rearrange("p (kt x) -> p kt x", kt=64)
                            [:, grp * 8:(grp + 1) * 8, 0:64],
                            pb[3][:, 0:512].rearrange(
                                "p (i x) -> p i x", i=8)), mark=True)
                    em.wait("t", "v", mvc)
                em.wait("t", "s", mqs)
                em.wait("t", "ld", l_mask)
                if self.debug_dump and c == 0 and h == 0:
                    self.dbg("a_qt", qt, 512, {"s": mqs})
                    self.dbg("a_qaug", qaug[0:33, :], 512, {"v": mba})
                    self.dbg("a_wtl", wtl, 1024, {"v": mwv})
                    self.dbg("a_ckvG", ckvG[:, 0:512], 512, {"ga": l_gath})
                    self.dbg("a_kaug", kaug[0:33, 0:512], 512,
                             {"ga": l_gath})
                    self.dbg("a_vh", vh[:, 0:520], 520, {"v": em.counts["v"]})
                pend = None
                for kt in range(nkt):
                    r_ = kt % 4
                    m_ = kt // 4
                    sb = pb[0] if kt % 2 == 0 else pb[1]
                    ebuf = expb[kt % 2]
                    em.wait("t", "s", sc_free[kt % 2])
                    self.mm(sb[:, 0:CH],
                            ckvG[:, r_ * TC + 128 * m_:
                                 r_ * TC + 128 * m_ + 128],
                            qt, True, False)
                    msc = self.mm(sb[:, 0:CH],
                                  kaug[0:96, r_ * TC + 128 * m_:
                                       r_ * TC + 128 * m_ + 128],
                                  qaug[0:96, 0:CH], False, True, mark=True)
                    if pend is not None:
                        peb, psrc, pval, pvt, pkt = pend
                        em.wait("t", psrc, pval)
                        exp_free[pkt % 2] = self.mm(
                            pb[2][0:65, 0:CH], pvt, peb,
                            start=(pkt == 0), stop=False, mark=True)
                    em.wait("s", "t", msc)
                    em.wait("s", "t", exp_free[kt % 2])
                    mex = self.sact(ebuf, sb[:, 0:CH], AF.Exp, mark=True)
                    sc_free[kt % 2] = mex
                    if m_ >= 4 * c:
                        i_ = m_ - 4 * c
                        em.wait("v", "s", mex)
                        mmk = self.vtt(ebuf, ebuf,
                                       maskA[:, (4 * i_ + r_) * CH:
                                             (4 * i_ + r_ + 1) * CH],
                                       ALU.mult, mark=True)
                        wsrc, wval = "v", mmk
                    else:
                        wsrc, wval = "s", mex
                    if (self.debug_dump and c == 0 and h == 0
                            and kt in (0, 5)):
                        self.dbg(f"a_exp{kt}", ebuf, 512,
                                 {wsrc: wval})
                    idx = (m_ % 4) * 4 + r_
                    cb = m_ // 4
                    vtile = vh[:, (16 * cb + idx) * 65:
                               (16 * cb + idx) * 65 + 65]
                    pend = (ebuf, wsrc, wval, vtile, kt)
                peb, psrc, pval, pvt, pkt = pend
                em.wait("t", psrc, pval)
                exp_free[pkt % 2] = self.mm(
                    pb[2][0:65, 0:CH], pvt, peb,
                    start=(pkt == 0), stop=True, mark=True)
                mavd = em.counts["t"]
                em.wait("s", "t", mavd)
                self.sact(linv[0:1, 0:CH], pb[2][64:65, 0:CH], AF.Ln)
                mlb = self.sact(linb[0:1, 0:CH], linv[0:1, 0:CH], AF.Exp,
                                scale=-1.0, mark=True)
                em.wait("t", "s", mlb)
                mbc = self.mm(pb[3][0:64, 0:CH], self.ones1b[0:1, 0:64],
                              linb[0:1, 0:CH], True, True, mark=True)
                em.wait("v", "t", mbc)
                self.vcopy(bcsb[0:64, 0:CH], pb[3][0:64, 0:CH], mark=True)
                dst = attn_all[:, (h // 2) * CH:(h // 2 + 1) * CH]
                dsth = dst[(h % 2) * 64:(h % 2) * 64 + 64, :]
                mat = self.vtt(dsth, pb[2][0:64, 0:CH], bcsb[0:64, 0:CH],
                               ALU.mult, mark=True)
                em.wait("t", "v", mat)
                if self.debug_dump and c == 0 and h == 0:
                    self.dbg("a_att0", dsth, 512, {"v": mat})
                    self.dbg("a_linv", linv[0:1, :], 512, {"v": mat})
                if h + 2 < NH:
                    prefetch_wq(h + 2)
            wo_res = {}
            for mt in range(8):
                slot = mt % 2
                em.wait("p", "t", wo_free[slot])
                wo_dma[slot] = em.dma(
                    "pq", wosl[slot].rearrange("p (a c) -> p a c", a=8),
                    d["woT"][:, mt].transpose([1, 0, 2]), sem=f"g{slot}")
                em.wait("t", f"g{slot}", wo_dma[slot])
                bank = pb[6] if mt % 2 == 0 else pb[7]
                if mt >= 2:
                    em.wait("t", "v", wo_res[mt - 2])
                for kt2 in range(8):
                    self.mm(bank[:, 0:CH],
                            wosl[slot][:, kt2 * 128:(kt2 + 1) * 128],
                            attn_all[:, kt2 * CH:(kt2 + 1) * CH],
                            start=(kt2 == 0), stop=(kt2 == 7),
                            mark=(kt2 == 7))
                mwo = em.counts["t"]
                wo_free[slot] = mwo
                em.wait("v", "t", mwo)
                xsl = self.X[:, mt * TC + t0: mt * TC + t0 + CH]
                wo_res[mt] = self.vtt(xsl, xsl, bank[:, 0:CH], ALU.add,
                                      mark=True)
            em.wait("t", "v", em.counts["v"])
            em.wait("s", "v", em.counts["v"])

    def _attn_b(self):
        em = self.em
        d = self.d
        ar = self.ar
        pb = self.pb
        ar.reset(self.prep_end)
        a = ar.alloc
        CH = 512

        nx2 = a(8 * TC, BF16)
        vh = a(64 * 65, BF16)
        kaugB = a(4 * TC, BF16)
        vtb = [a(TC, BF16) for _ in range(2)]
        attn_allB = a(4 * 8 * CH, BF16)
        tabQ = a(TC, BF16)
        tabK = a(TC, BF16)
        wqh = [a(8 * 64, BF16) for _ in range(2)]
        wkh = [a(8 * 64, BF16) for _ in range(2)]
        wvh = [a(8 * 64, BF16) for _ in range(2)]
        wosl = [a(8 * 128, BF16) for _ in range(2)]
        bout = a(8, F32)
        qaug = a(CH, BF16)
        ksb = a(CH, BF16)
        vsb = a(CH, BF16)
        expb = [a(CH, BF16) for _ in range(2)]
        bcsb = a(CH, BF16)
        linb = a(CH, BF16)
        xb = [a(CH, BF16), a(CH, BF16)]
        xsq = [a(CH, BF16), a(CH, BF16)]
        rows = a(5 * CH, F32)
        scr = a(CH, F32)
        self._rp_scr1 = a(CH, F32)
        self._rp_scr2 = a(CH, F32)
        b2row = rows[0:1, 0:CH]
        linv = rows[0:1, CH:2 * CH]
        mkrow = a(8, F32)
        onesneg = expb[0]
        vones = expb[1]

        em.dma("pq", tabQ[0:32, :], d["cosBq"], sem="ld")
        em.dma("pq", tabQ[32:64, :], d["sinBq"], sem="ld")
        em.dma("pq", tabK[0:32, :], d["cosBk"], sem="ld")
        em.dma("pq", tabK[32:64, :], d["sinBk"], sem="ld")
        l_bout = em.dma("pq", bout, d["bout"], sem="ld")
        l_tq2 = l_tk2 = l_bout
        lnbufs = (xb, xsq, rows, scr)
        cc2 = d["cc2_in"].ap()
        self.vmemset(kaugB[64:96, :], 0.0)
        self.vmemset(
            vh.rearrange("p (kt x) -> p kt x", kt=64)[:, :, 64:65], 1.0)
        m_zkb = self.vmemset(qaug[64:96, 0:CH], 0.0, mark=True)
        em.wait("t", "v", m_zkb)
        mno = self.vmemset(onesneg[0:16, :], -1.0, mark=True)
        mvo = self.vmemset(vones[0:16, :], 1.0, mark=True)
        em.wait("q", "v", mvo)
        for cq in range(4):
            em.dma("sq", cc2[64:1040:65, cq * CH:(cq + 1) * CH],
                   onesneg[0:16, :], sem="cs")
            em.dma("sq", cc2[1040 + 64:2080:65, cq * CH:(cq + 1) * CH],
                   vones[0:16, :], sem="cs")

        kv_dma = [0, 0]
        kv_free = [0, 0]

        def prefetch_kv(h):
            slot = h % 2
            em.wait("p", "t", kv_free[slot])
            em.dma("pq", wkh[slot].rearrange("p (a c) -> p a c", a=8),
                   d["wkh"][h].transpose([1, 0, 2]), sem=f"e{slot}")
            kv_dma[slot] = em.dma(
                "pq", wvh[slot].rearrange("p (a c) -> p a c", a=8),
                d["wvh"][h].transpose([1, 0, 2]), sem=f"e{slot}")

        for c in range(4):
            self._attn_ln(c * CH, CH, nx2, TC, c * CH, lnbufs)
        kv_st = 0
        prefetch_kv(0)
        prefetch_kv(1)
        for c in range(4):
            t0 = c * CH
            for h in range(NH):
                slot = h % 2
                em.wait("t", f"e{slot}", kv_dma[slot])
                em.wait("t", "v", getattr(self, "_kv_prev", 0))
                for kt in range(8):
                    self.mm(pb[0][0:64, 0:CH],
                            wkh[slot][:, kt * 64:(kt + 1) * 64],
                            nx2[:, kt * TC + t0:kt * TC + t0 + CH],
                            start=(kt == 0), stop=(kt == 7), mark=(kt == 7))
                mk_ = em.counts["t"]
                for kt in range(8):
                    self.mm(pb[1][0:64, 0:CH],
                            wvh[slot][:, kt * 64:(kt + 1) * 64],
                            nx2[:, kt * TC + t0:kt * TC + t0 + CH],
                            start=(kt == 0), stop=(kt == 7), mark=(kt == 7))
                mv_ = em.counts["t"]
                kv_free[slot] = mv_
                em.wait("v", "t", mk_)
                em.wait("v", "ld", l_tk2)
                em.wait("v", "cs", kv_st)
                self._rope_apply(pb[0][0:32, 0:CH], pb[0][32:64, 0:CH],
                                 tabK, t0, ksb[0:32, 0:CH],
                                 ksb[32:64, 0:CH], CH, 32)
                em.wait("v", "t", mv_)
                mvs = self.vcopy(vsb[0:64, 0:CH], pb[1][0:64, 0:CH],
                                 mark=True)
                self._kv_prev = mvs
                em.wait("q", "v", mvs)
                if self.debug_dump and c == 0 and h == 0:
                    self.dbg("b_ksb", ksb[0:64, 0:CH], 512,
                             {"v": em.counts["v"]})
                    self.dbg("b_vsb", vsb[0:64, 0:CH], 512, {"v": mvs})
                em.dma("sq", cc2[65 * h:65 * h + 64, t0:t0 + CH],
                       ksb[0:64, 0:CH], sem="cs")
                kv_st = em.dma("sq", cc2[1040 + 65 * h:1040 + 65 * h + 64,
                                         t0:t0 + CH], vsb[0:64, 0:CH],
                               sem="cs")
                if h + 2 < NH:
                    prefetch_kv(h + 2)
            if c + 1 < 4:
                prefetch_kv(0)
                prefetch_kv(1)
        em.wait("p", "cs", kv_st)

        def ccfn(g):
            return g.collective_compute(
                "AllGather", ALU.bypass,
                replica_groups=[[0, 1, 2, 3], [4, 5, 6, 7]],
                ins=[d["cc2_in"].ap().opt()], outs=[d["cc2_out"].ap().opt()])
        em.collective(ccfn, "cc2")
        em.wait("p", "cc2", 1)
        cc2o = d["cc2_out"].ap()
        if self.ser & 64:
            return

        wq_dma2 = [0, 0]
        wq_free2 = [0, 0]

        def prefetch_wq(h):
            slot = h % 2
            em.wait("p", "t", wq_free2[slot])
            wq_dma2[slot] = em.dma(
                "pq", wqh[slot].rearrange("p (a c) -> p a c", a=8),
                d["wqh"][h].transpose([1, 0, 2]), sem=f"h{slot}")

        prefetch_wq(0)
        prefetch_wq(1)
        exp_free = [0, 0]
        sc_free = [0, 0]
        vt_dma = [0, 0]
        vt_free = [0, 0]
        ka_dma = 0
        nheads = 1 if (self.ser & 128) else NH
        for h in range(nheads):
            slot = h % 2
            prev_ka = ka_dma
            em.wait("p", "t", em.counts["t"] if h > 0 else 0)
            em.wait("p", "v", m_zkb)
            for r in range(4):
                ka_dma = em.dma("pq", kaugB[0:65, r * TC:(r + 1) * TC],
                                cc2o[2080 * r + 65 * h:
                                     2080 * r + 65 * h + 65, :], sem="ga")
            em.wait("s", "ga", ka_dma)
            for r in range(4):
                em.wait("t", "v", em.counts["v"])
                mk_m = 0
                for cc_ in range(4):
                    sl = cc_ % 2
                    if cc_ >= 2:
                        em.wait("s", "t", mk_m)
                    m1_ = self.sact(xsq[sl][0:65, :],
                                    kaugB[0:65, r * TC + cc_ * CH:
                                          r * TC + (cc_ + 1) * CH],
                                    AF.Square, mark=True)
                    em.wait("t", "s", m1_)
                    mk_m = self.mm(pb[3][0:1, 0:CH], self.onescb[0:65, :],
                                   xsq[sl][0:65, :], start=(cc_ == 0),
                                   stop=(cc_ == 3), mark=True)
                em.wait("v", "t", mk_m)
                em.careful = True
                self.vred(rows[0:1, 0:1], pb[3][0:1, 0:CH], ALU.max)
                self.vcopy(mkrow[0:1, r:r + 1], rows[0:1, 0:1], mark=True)
                em.careful = False
            em.careful = True
            self.vred(mkrow[0:1, 4:5], mkrow[0:1, 0:4], ALU.max, mark=True)
            mks = em.counts["v"]
            em.wait("s", "v", mks)
            self.sact(mkrow[0:1, 5:6], mkrow[0:1, 4:5], AF.Ln)
            mkf = self.sact(mkrow[0:1, 6:7], mkrow[0:1, 5:6], AF.Exp,
                            scale=0.5, mark=True)
            em.careful = False
            em.wait("t", "v", em.counts["v"])
            for r in range(4) if not (self.ser & 512) else []:
                vslot = r % 2
                em.wait("p", "t", vt_free[vslot])
                vt_dma[vslot] = em.dma(
                    "pq", vtb[vslot][0:65, :],
                    cc2o[2080 * r + 1040 + 65 * h:
                         2080 * r + 1040 + 65 * h + 65, :], sem=f"g{vslot}")
                em.wait("t", f"g{vslot}", vt_dma[vslot])
                for grp in range(2):
                    for ii in range(8):
                        mtile = grp * 8 + ii
                        self.tr(self.pbb[3][:, ii * 64:ii * 64 + 64],
                                vtb[vslot][0:64, 128 * mtile:
                                           128 * mtile + 128],
                                mark=(ii == 7))
                    mtr = em.counts["t"]
                    vt_free[vslot] = mtr
                    em.wait("v", "t", mtr)
                    mvc = self.em.v(
                        lambda v, r=r, grp=grp:
                        v.tensor_copy(
                            vh.rearrange("p (kt x) -> p kt x", kt=64)
                            [:, r * 16 + grp * 8:r * 16 + (grp + 1) * 8,
                             0:64],
                            self.pbb[3][:, 0:512].rearrange(
                                "p (i x) -> p i x", i=8)), mark=True)
                    em.wait("t", "v", mvc)
            em.wait("t", f"h{slot}", wq_dma2[slot])
            for c in range(4):
                t0 = c * CH
                em.wait("t", "v", getattr(self, "_qb_prev", 0))
                for kt in range(8):
                    self.mm(pb[4][0:64, 0:CH],
                            wqh[slot][:, kt * 64:(kt + 1) * 64],
                            nx2[:, kt * TC + t0:kt * TC + t0 + CH],
                            start=(kt == 0), stop=(kt == 7), mark=(kt == 7))
                mq_ = em.counts["t"]
                em.wait("v", "t", mq_)
                em.wait("v", "ld", l_tq2)
                mrope = self._rope_apply(pb[4][0:32, 0:CH],
                                         pb[4][32:64, 0:CH], tabQ, t0,
                                         qaug[0:32, 0:CH],
                                         qaug[32:64, 0:CH], CH, 32)
                self._qb_prev = mrope
                em.wait("s", "v", mrope)
                s1 = self.sact(xsq[0][0:64, :], qaug[0:64, 0:CH],
                               AF.Square, mark=True)
                em.wait("t", "s", s1)
                em.wait("t", "v", mrope)
                self.mm(pb[5][0:1, 0:CH], self.onescb[0:64, :],
                        xsq[0][0:64, :], True, True, mark=True)
                mbn = em.counts["t"]
                em.wait("v", "t", mbn)
                em.careful = True
                self._qb_prev = max(self._qb_prev, self.vcopy(
                    b2row[0:1, 0:CH], pb[5][0:1, 0:CH], mark=True))
                em.wait("s", "v", em.counts["v"])
                self.sact(b2row[0:1, 0:CH], b2row[0:1, 0:CH], AF.Ln)
                mbs = self.sact(b2row[0:1, 0:CH], b2row[0:1, 0:CH], AF.Exp,
                                scale=0.5, mark=True)
                em.wait("v", "s", mbs)
                em.wait("v", "s", mkf)
                self.vtsap(b2row[0:1, 0:CH], b2row[0:1, 0:CH],
                           mkrow[0:1, 6:7], ALU.mult)
                self.vts(b2row[0:1, 0:CH], b2row[0:1, 0:CH], 1.02, ALU.mult)
                mba = self.vcopy(qaug[64:65, 0:CH], b2row[0:1, 0:CH],
                                 mark=True)
                em.careful = False
                em.wait("t", "v", mba)
                if self.debug_dump and c == 0 and h == 0:
                    self.dbg("b_qaug", qaug[0:65, 0:CH], 512, {"v": mba})
                    self.dbg("b_kaug", kaugB[0:65, 0:512], 512,
                             {"ga": ka_dma})
                    self.dbg("b_vh", vh[:, 0:65], 65, {"v": em.counts["v"]})
                    self.dbg("b_nx2", nx2[:, 0:512], 512,
                             {"v": em.counts["v"]})
                pend = None
                for kt in range(64) if not (self.ser & 256) else []:
                    r_ = kt // 16
                    m_ = kt % 16
                    sb = pb[0] if kt % 2 == 0 else pb[1]
                    ebuf = expb[kt % 2]
                    em.wait("t", "s", sc_free[kt % 2])
                    msc = self.mm(sb[:, 0:CH],
                                  kaugB[0:96, r_ * TC + 128 * m_:
                                        r_ * TC + 128 * m_ + 128],
                                  qaug[0:96, 0:CH], True, True, mark=True)
                    if pend is not None:
                        peb, pval, pvt, pkt = pend
                        em.wait("t", "s", pval)
                        exp_free[pkt % 2] = self.mm(
                            pb[2][0:65, 0:CH], pvt, peb,
                            start=(pkt == 0), stop=False, mark=True)
                    em.wait("s", "t", msc)
                    em.wait("s", "t", exp_free[kt % 2])
                    mex = self.sact(ebuf, sb[:, 0:CH], AF.Exp, mark=True)
                    sc_free[kt % 2] = mex
                    if (self.debug_dump and c == 0 and h == 0
                            and kt in (0, 17)):
                        self.dbg(f"b_exp{kt}", ebuf, 512, {"s": mex})
                    pend = (ebuf, mex, vh[:, kt * 65:kt * 65 + 65], kt)
                if self.ser & 256:
                    continue
                peb, pval, pvt, pkt = pend
                em.wait("t", "s", pval)
                exp_free[pkt % 2] = self.mm(
                    pb[2][0:65, 0:CH], pvt, peb,
                    start=(pkt == 0), stop=True, mark=True)
                mavd = em.counts["t"]
                em.wait("s", "t", mavd)
                self.sact(linv[0:1, 0:CH], pb[2][64:65, 0:CH], AF.Ln)
                mlb = self.sact(linb[0:1, 0:CH], linv[0:1, 0:CH], AF.Exp,
                                scale=-1.0, mark=True)
                em.wait("t", "s", mlb)
                mbc = self.mm(pb[3][0:64, 0:CH], self.ones1b[0:1, 0:64],
                              linb[0:1, 0:CH], True, True, mark=True)
                em.wait("v", "t", mbc)
                self.vcopy(bcsb[0:64, 0:CH], pb[3][0:64, 0:CH], mark=True)
                dst = attn_allB[:, (c * 8 + h // 2) * CH:
                                (c * 8 + h // 2 + 1) * CH]
                dsth = dst[(h % 2) * 64:(h % 2) * 64 + 64, :]
                mat = self.vtt(dsth, pb[2][0:64, 0:CH], bcsb[0:64, 0:CH],
                               ALU.mult, mark=True)
                em.wait("t", "v", mat)
                if self.debug_dump and c == 0 and h == 0:
                    self.dbg("b_att0", dsth, 512, {"v": mat})
                    self.dbg("b_linv", linv[0:1, :], 512, {"v": mat})
            wq_free2[slot] = em.counts["t"]
            if h + 2 < nheads:
                prefetch_wq(h + 2)
        if self.ser & (128 | 256 | 512):
            return

        wo_dma = [0, 0]
        wo_free = [0, 0]
        em.wait("v", "ld", l_bout)
        em.wait("t", "v", em.counts["v"])
        wo_res = {}
        for c in range(4):
            t0 = c * CH
            for mt in range(8):
                slot = mt % 2
                em.wait("p", "t", wo_free[slot])
                wo_dma[slot] = em.dma(
                    "pq", wosl[slot].rearrange("p (a c) -> p a c", a=8),
                    d["woutT"][:, mt].transpose([1, 0, 2]), sem=f"e{slot}")
                em.wait("t", f"e{slot}", wo_dma[slot])
                bank = pb[6] if mt % 2 == 0 else pb[7]
                key = (c, mt)
                pk = (c, mt - 2) if mt >= 2 else (c - 1, 6 + mt)
                if pk in wo_res:
                    em.wait("t", "v", wo_res[pk])
                for kt2 in range(8):
                    self.mm(bank[:, 0:CH],
                            wosl[slot][:, kt2 * 128:(kt2 + 1) * 128],
                            attn_allB[:, (c * 8 + kt2) * CH:
                                      (c * 8 + kt2 + 1) * CH],
                            start=(kt2 == 0), stop=(kt2 == 7),
                            mark=(kt2 == 7))
                mwo = em.counts["t"]
                wo_free[slot] = mwo
                em.wait("v", "t", mwo)
                xsl = self.X[:, mt * TC + t0: mt * TC + t0 + CH]
                wo_res[key] = self.vstt(xsl, bank[:, 0:CH],
                                        bout[:, mt:mt + 1], xsl,
                                        ALU.add, ALU.add, mark=True)


def em_replay_one(em, ename, eobj, sems):
    for item in em.streams[ename]:
        if item[0] == "op":
            _, fn, sem_name, inc = item
            ins = fn(eobj)
            if sem_name is not None:
                if inc is None:
                    ins.then_inc(sems[sem_name])
                else:
                    ins.then_inc(sems[sem_name], inc)
        else:
            _, src, val = item
            eobj.wait_ge(sems[src], val)



def _f32(x):
    return np.ascontiguousarray(np.asarray(x), dtype=np.float32)


def host_ffn_inputs(inp):
    w1 = _f32(inp["L0_ffn_w1"])
    w3 = _f32(inp["L0_ffn_w3"])
    w2 = _f32(inp["L0_ffn_w2"])
    gate = _f32(inp["L0_ffn_gate_w"])
    a1 = _f32(inp["L0_ffn_a1"])
    b1 = _f32(inp["L0_ffn_b1"])
    a3 = _f32(inp["L0_ffn_a3"])
    b3 = _f32(inp["L0_ffn_b3"])
    a2 = _f32(inp["L0_ffn_a2"])
    b2 = _f32(inp["L0_ffn_b2"])
    nw = _f32(inp["L0_ffn_norm_w"])
    d = {}
    d["w1p"] = np.ascontiguousarray(
        w1.T.reshape(8, 128, 16, 256).transpose(2, 0, 1, 3))
    d["w3p"] = np.ascontiguousarray(
        w3.T.reshape(8, 128, 16, 256).transpose(2, 0, 1, 3))
    d["w2t"] = np.ascontiguousarray(
        w2.T.reshape(32, 128, 8, 128).transpose(2, 0, 1, 3))
    d["gateT"] = np.ascontiguousarray(gate.T.reshape(8, 128, 8))
    d["a1T"] = np.ascontiguousarray(
        a1.reshape(NE * RK, D).T.reshape(8, 128, 128))
    d["a3T"] = np.ascontiguousarray(
        a3.reshape(NE * RK, D).T.reshape(8, 128, 128))
    d["a2T"] = np.ascontiguousarray(
        a2.reshape(NE * RK, FF).T.reshape(32, 128, 128))
    d["b1T"] = np.ascontiguousarray(b1.transpose(0, 2, 1).reshape(NE * RK, FF))
    d["b3T"] = np.ascontiguousarray(b3.transpose(0, 2, 1).reshape(NE * RK, FF))
    b2s = b2.transpose(0, 2, 1).reshape(NE * RK, D)
    bE = np.zeros((128, D), np.float32)
    bO = np.zeros((128, D), np.float32)
    for e in range(NE):
        blk = b2s[16 * e:16 * (e + 1)]
        if e % 2 == 0:
            bE[16 * e:16 * e + 16] = blk
        else:
            bO[16 * (e - 1):16 * (e - 1) + 16] = blk
    d["b2SE"] = bE
    d["b2SO"] = bO
    d["normw"] = np.ascontiguousarray(nw.reshape(8, 128).T)
    sE = np.zeros((8, 128), np.float32)
    sO = np.zeros((8, 128), np.float32)
    for e in range(NE):
        if e % 2 == 0:
            sE[e, 16 * e:16 * e + 16] = 1.0
        else:
            sO[e, 16 * (e - 1):16 * (e - 1) + 16] = 1.0
    d["selE"] = sE
    d["selO"] = sO
    bsel = np.zeros((8, 8, 128), np.float32)
    for e in range(NE):
        bsel[e, e, :] = 1.0
    d["bsel"] = bsel.reshape(8 * 8, 128).reshape(8, 8 * 128) if False \
        else np.ascontiguousarray(bsel.transpose(1, 0, 2).reshape(8, 8 * 128))
    d["identf"] = np.eye(128, dtype=np.float32)
    return d


def host_attn_shared(inp):
    d = {}
    wkvd = _f32(inp["L0_attn_a_W_kv_down"])
    d["wkvd"] = np.ascontiguousarray(wkvd.reshape(8, 128, 128))
    wq = _f32(inp["L0_attn_a_W_q"])
    d["wqTt"] = np.ascontiguousarray(
        wq.T.reshape(16, 64, 8, 128).transpose(0, 2, 1, 3))
    d["wkupT"] = np.ascontiguousarray(
        _f32(inp["L0_attn_a_W_k_up"]).T.reshape(16, 64, 128))
    d["wvup"] = _f32(inp["L0_attn_a_W_v_up"])
    d["wqrope"] = np.ascontiguousarray(
        _f32(inp["L0_attn_a_W_q_rope"]).reshape(8, 128, 32))
    d["wkrope"] = _f32(inp["L0_attn_a_W_k_rope"])
    d["woT"] = np.ascontiguousarray(
        _f32(inp["L0_attn_a_W_o"]).reshape(8, 128, 8, 128).transpose(
            0, 2, 1, 3))
    wqkv = _f32(inp["L0_attn_b_w_qkv"])
    for nm, off in [("wqh", 0), ("wkh", 1024), ("wvh", 2048)]:
        d[nm] = np.ascontiguousarray(
            wqkv[:, off:off + 1024].reshape(8, 128, 16, 64).transpose(
                2, 0, 1, 3))
    d["woutT"] = np.ascontiguousarray(
        _f32(inp["L0_attn_b_w_out"]).reshape(8, 128, 8, 128).transpose(
            0, 2, 1, 3))
    d["bout"] = np.ascontiguousarray(
        _f32(inp["L0_attn_b_b_out"]).reshape(8, 128).T)
    return d


def host_attn_core(core):
    b, j = core // 4, core % 4
    pos = (4.0 * np.arange(TC) + j).astype(np.float64)
    d = {}
    invfA = (10000.0 ** (-np.arange(0, DR, 2) / DR)).astype(np.float64)
    angA = pos[None, :] * invfA[:, None]
    d["cosAq"] = (np.cos(angA) / SCALE_A).astype(np.float32)
    d["sinAq"] = (np.sin(angA) / SCALE_A).astype(np.float32)
    d["cosAk"] = np.cos(angA).astype(np.float32)
    d["sinAk"] = np.sin(angA).astype(np.float32)
    invfB = (10000.0 ** (-np.arange(32) / 32.0)).astype(np.float64)
    angB = pos[None, :] * invfB[:, None]
    d["cosBq"] = (np.cos(angB) * SCALE_B).astype(np.float32)
    d["sinBq"] = (np.sin(angB) * SCALE_B).astype(np.float32)
    d["cosBk"] = np.cos(angB).astype(np.float32)
    d["sinBk"] = np.sin(angB).astype(np.float32)
    mask = np.zeros((128, 16, 512), np.float32)
    p = np.arange(128)[:, None]
    f = np.arange(512)[None, :]
    for i in range(4):
        for r in range(4):
            mask[:, 4 * i + r, :] = (512 * i + 4 * p + r <= 4 * f + j)
    d["maskA"] = np.ascontiguousarray(mask.reshape(128, 16 * 512))
    return d


def host_shard_x(x, core):
    b, j = core // 4, core % 4
    xs = np.asarray(x, dtype=np.float32)[b, j::4, :]
    xf = np.ascontiguousarray(xs.T)
    return np.ascontiguousarray(
        xf.reshape(8, 128, TC).transpose(1, 0, 2).reshape(128, 8 * TC))


def host_unshard_y(shards):
    y = np.empty((B, T, D), dtype=np.float32)
    for c in range(NCORES):
        b, j = c // 4, c % 4
        y[b, j::4, :] = shards[c]
    return y


def kernel(**inputs):
    bld = Builder()
    nc = bld.build()
    shared = host_ffn_inputs(inputs)
    shared.update(host_attn_shared(inputs))
    in_maps = []
    for c in range(NCORES):
        m = dict(shared)
        m.update(host_attn_core(c))
        m["x_fm"] = host_shard_x(inputs["L0_attn_a_x"], c)
        in_maps.append(m)
    res = run_bass_kernel_spmd(nc, in_maps, core_ids=list(range(NCORES)))
    shards = [res.results[c]["y_tok"] for c in range(NCORES)]
    return host_unshard_y(shards)


# version 07
# speedup vs baseline: 1.6328x; 1.1234x over previous
import numpy as np
import ml_dtypes
import concourse.bass as bass
import concourse.mybir as mybir
from concourse.bass_utils import run_bass_kernel_spmd

F32 = mybir.dt.float32
BF16 = mybir.dt.bfloat16
U8 = mybir.dt.uint8
AF = mybir.ActivationFunctionType
ALU = mybir.AluOpType
AX = mybir.AxisListType

B, T, D = 2, 8192, 1024
TC = 2048
NH, HD = 16, 64
DC, DR = 128, 32
FF, NE, RK = 4096, 8, 16
SCALE_A = float(np.sqrt(HD + DR))
SCALE_B = HD ** (-0.5)
EPS_LN = 1e-5
EPS_RMS = 1e-6
SCALING = 2.0
NCORES = 8
ARENA_BYTES = 212600


class Em:
    ENGS = ("t", "v", "s", "p", "q")
    DSEMS = ("pq", "sq", "ld", "lx", "e0", "e1", "g0", "g1", "h0",
             "h1", "wd", "qa", "ga", "cs", "db")

    def __init__(self):
        self.streams = {e: [] for e in self.ENGS}
        self.counts = {e: 0 for e in self.ENGS}
        self.dma_counts = {s: 0 for s in self.DSEMS}
        self.watermark = {e: {} for e in self.ENGS}

    careful = False

    def _op(self, eng, fn, mark):
        if self.careful and eng == "v":
            mark = True
        if mark:
            self.counts[eng] += 1
            self.streams[eng].append(("op", fn, eng, 1))
        else:
            self.streams[eng].append(("op", fn, None, 0))
        if self.careful and eng == "v":
            self.wait("v", "v", self.counts["v"])
        return self.counts[eng]

    def t(self, fn, mark=False):
        return self._op("t", fn, mark)

    def v(self, fn, mark=False):
        return self._op("v", fn, mark)

    def s(self, fn, mark=False):
        return self._op("s", fn, mark)

    def p(self, fn, mark=False):
        return self._op("p", fn, mark)

    def q(self, fn, mark=False):
        return self._op("q", fn, mark)

    def collective(self, fn, cc_name):
        self.streams["p"].append(("op", fn, cc_name, None))

    def dma(self, q, out, in_, sem=None, **kw):
        eng = "p" if q == "pq" else "q"
        sem = sem or q
        self.dma_counts[sem] += 16
        val = self.dma_counts[sem]

        def fn(e, out=out, in_=in_, kw=kw):
            return e.dma_start(out, in_, **kw)
        self.streams[eng].append(("op", fn, sem, 16))
        return val

    def wait(self, eng, src, val):
        if val is None or val <= 0:
            return
        wm = self.watermark[eng]
        if wm.get(src, 0) >= val:
            return
        wm[src] = val
        self.streams[eng].append(("wait", src, val))

    def waits(self, eng, marks):
        for src, val in marks.items():
            self.wait(eng, src, val)

    def snapshot(self):
        d = dict(self.counts)
        d.update(self.dma_counts)
        return d

    def full_barrier(self):
        snap = self.snapshot()
        for e in self.ENGS:
            self.waits(e, snap)

    def replay(self, engine_objs, sems):
        for ename, stream in self.streams.items():
            eobj = engine_objs[ename]
            for item in stream:
                if item[0] == "op":
                    _, fn, sem_name, inc = item
                    ins = fn(eobj)
                    if sem_name is not None:
                        if inc is None:
                            ins.then_inc(sems[sem_name])
                        else:
                            ins.then_inc(sems[sem_name], inc)
                else:
                    _, src, val = item
                    eobj.wait_ge(sems[src], val)


class Arena:
    def __init__(self, ap_u8, nbytes):
        self.base = ap_u8
        self.nbytes = nbytes
        self.off = 0
        self.peak = 0

    def alloc(self, cols, dt, align=64):
        size = {F32: 4, BF16: 2, U8: 1}[dt]
        self.off = (self.off + align - 1) // align * align
        start = self.off
        self.off += cols * size
        self.peak = max(self.peak, self.off)
        assert self.off <= self.nbytes, \
            f"arena overflow: {self.off} > {self.nbytes}"
        return self.base[:, start:start + cols * size].bitcast(dt)

    def mark(self):
        return self.off

    def reset(self, m):
        self.off = m


class Builder:

    def __init__(self, phases=("attn_a", "ffn1", "attn_b", "ffn2"),
                 debug_dump=False, ser=0, arena_fill=False):
        self.ser = ser
        self.arena_fill = arena_fill
        self.debug_dump = debug_dump
        self.dbg_map = {}
        self.dbg_off = 0
        self.phases = phases
        nc = bass.Bass("TRN2", target_bir_lowering=False, debug=False)
        self.nc = nc
        self.em = Em()
        self.d = {}
        self._declare_dram()
        self._ctxs = []
        arena_t = nc.sbuf_tensor("arena", [128, ARENA_BYTES], U8)
        self._ctxs.append(arena_t)
        self._arena_t = arena_t

    def _declare_dram(self):
        nc = self.nc
        d = self.d

        def din(name, shape, dt=F32):
            d[name] = nc.dram_tensor(name, list(shape), dt,
                                     kind="ExternalInput").ap()

        def dscr(name, shape, dt):
            d[name] = nc.dram_tensor(name, list(shape), dt)

        din("x_fm", (128, 8 * TC))
        d["dbg"] = nc.dram_tensor("dbg", [128, 40960], F32,
                                  kind="ExternalOutput").ap()
        d["y_tok"] = nc.dram_tensor("y_tok", [TC, D], F32,
                                    kind="ExternalOutput").ap()
        din("identf", (128, 128))
        din("selE", (8, 128))
        din("selO", (8, 128))
        din("bsel", (8, 8 * 128))
        din("w1p", (16, 8, 128, 256))
        din("w3p", (16, 8, 128, 256))
        din("w2t", (8, 32, 128, 128))
        din("gateT", (8, 128, 8))
        din("a1T", (8, 128, 128))
        din("a3T", (8, 128, 128))
        din("a2T", (32, 128, 128))
        din("b1T", (128, FF))
        din("b3T", (128, FF))
        din("b2SE", (128, D))
        din("b2SO", (128, D))
        din("normw", (128, 8))
        din("wkvd", (8, 128, 128))
        din("wqTt", (16, 8, 64, 128))
        din("wkupT", (16, 64, 128))
        din("wvup", (128, 1024))
        din("wqrope", (8, 128, 32))
        din("wkrope", (128, 32))
        din("woT", (8, 8, 128, 128))
        din("cosAq", (16, TC))
        din("sinAq", (16, TC))
        din("cosAk", (16, TC))
        din("sinAk", (16, TC))
        din("maskA", (128, 16 * 512))
        din("wqh", (16, 8, 128, 64))
        din("wkh", (16, 8, 128, 64))
        din("wvh", (16, 8, 128, 64))
        din("woutT", (8, 8, 128, 128))
        din("bout", (128, 8))
        din("cosBq", (32, TC))
        din("sinBq", (32, TC))
        din("cosBk", (32, TC))
        din("sinBk", (32, TC))
        dscr("cc1_in", [161, TC], BF16)
        dscr("cc1_out", [644, TC], BF16)
        dscr("cc2_in", [2080, TC], BF16)
        dscr("cc2_out", [8320, TC], BF16)
        dscr("w1bf", [16, 8, 128, 256], BF16)
        dscr("w3bf", [16, 8, 128, 256], BF16)
        dscr("w2bf", [8, 32, 128, 128], BF16)
        dscr("chd", [16, 4, 128, 256], BF16)

    def build(self):
        nc = self.nc
        import contextlib
        stack = contextlib.ExitStack()
        arena_h = stack.enter_context(self._arena_t)
        pb_h = [stack.enter_context(nc.psum_tensor(f"pb{i}", [128, 512], F32))
                for i in range(8)]
        sem_names = ["t", "v", "s", "p", "q", "cc1", "cc2"] + list(Em.DSEMS)
        sems = {n: stack.enter_context(nc.semaphore(f"sem_{n}"))
                for n in sem_names}
        block = stack.enter_context(nc.Block())

        self.ar = Arena(arena_h.ap() if hasattr(arena_h, "ap") else
                        arena_h[:], ARENA_BYTES)
        self.pb = [p[:] for p in pb_h]
        self.pbb = [p[:].bitcast(BF16) for p in pb_h]

        self._alloc_persistent()
        self._emit_all()

        em = self.em
        streams_done = em

        @block.tensor
        def _(t):
            em_replay_one(em, "t", t, sems)

        @block.vector
        def _(v):
            em_replay_one(em, "v", v, sems)

        @block.scalar
        def _(s):
            em_replay_one(em, "s", s, sems)

        @block.gpsimd
        def _(g):
            em_replay_one(em, "p", g, sems)

        @block.sync
        def _(q):
            em_replay_one(em, "q", q, sems)

        stack.close()
        return nc

    def _alloc_persistent(self):
        a = self.ar.alloc
        self.X = a(8 * TC, F32)
        self.ones1b = a(128, BF16)
        self.ones1f = a(128, F32)
        self.onescb = a(1, BF16)
        self.epsc = a(4, F32)
        self.identb = a(128, BF16)
        self.identf = a(128, F32)
        self.persist_mark = self.ar.mark()

    def dbg(self, name, ap, cols, prod_marks):
        if not self.debug_dump:
            return
        em = self.em
        for src_, val in prod_marks.items():
            em.wait("p", src_, val)
        npart = ap.shape[0]
        dst = self.d["dbg"][0:npart, self.dbg_off:self.dbg_off + cols]
        dv = em.dma("pq", dst, ap, sem="db")
        em.wait("v", "db", dv)
        em.wait("s", "db", dv)
        em.wait("t", "db", dv)
        self.dbg_map[name] = (self.dbg_off, cols)
        self.dbg_off += cols
        assert self.dbg_off <= 40960

    def mm(self, out, lhsT, rhs, start, stop, mark=False, tp=None):
        def fn(t, out=out, lhsT=lhsT, rhs=rhs, start=start, stop=stop, tp=tp):
            if tp is None:
                return t.matmul(out, lhsT, rhs, start=start, stop=stop)
            return t.matmul(out, lhsT, rhs, start=start, stop=stop,
                            tile_position=tp)
        return self.em.t(fn, mark=mark)

    def tr(self, out, in_, mark=False, fp32=False):
        k = in_.shape[0]
        ident = self.identf if fp32 else self.identb
        ident = ident[0:k, 0:k]
        return self.em.t(lambda t, out=out, in_=in_, i=ident:
                         t.transpose(out, in_, i), mark=mark)

    def vtt(self, out, a, b, op, mark=False):
        return self.em.v(lambda v, out=out, a=a, b=b, op=op:
                         v.tensor_tensor(out, a, b, op=op), mark=mark)

    def vstt(self, out, in0, scalar, in1, op0, op1, mark=False):
        return self.em.v(
            lambda v, out=out, in0=in0, s=scalar, in1=in1, o0=op0, o1=op1:
            v.scalar_tensor_tensor(out, in0, s, in1, op0=o0, op1=o1),
            mark=mark)

    def vcopy(self, out, in_, mark=False):
        return self.em.v(lambda v, out=out, in_=in_: v.tensor_copy(out, in_),
                         mark=mark)

    def vtsap(self, out, in_, sap, op, mark=False):
        return self.em.v(lambda v, out=out, in_=in_, s1=sap, op=op:
                         v.tensor_scalar(out, in_, s1, None, op0=op),
                         mark=mark)

    def vts(self, out, in_, s1, op, mark=False):
        return self.em.v(lambda v, out=out, in_=in_, s1=s1, op=op:
                         v.tensor_scalar(out, in_, s1, None, op0=op),
                         mark=mark)

    def vrecip(self, out, in_, mark=False):
        m = self.em.v(lambda v, out=out, in_=in_: v.reciprocal(out, in_),
                      mark=True)
        self.em.wait("v", "v", m)
        return m

    def tree8(self, out_col, grp, scratch, op, mark=False):
        self.vtt(scratch[:, 0:4], grp[:, 0:4], grp[:, 4:8], op)
        self.vtt(scratch[:, 4:6], scratch[:, 0:2], scratch[:, 2:4], op)
        return self.vtt(out_col, scratch[:, 4:5], scratch[:, 5:6], op,
                        mark=mark)

    def vred(self, out, in_, op, mark=False):
        m = self.em.v(lambda v, out=out, in_=in_, op=op:
                      v.tensor_reduce(out, in_, axis=AX.X, op=op),
                      mark=True)
        self.em.wait("v", "v", m)
        return m

    def vmemset(self, ap, val, mark=False):
        return self.em.v(lambda v, ap=ap, val=val: v.memset(ap, val),
                         mark=mark)

    def sact(self, out, in_, func, bias=None, scale=None, mark=False):
        def fn(s, out=out, in_=in_, func=func, bias=bias, scale=scale):
            kw = {}
            if bias is not None:
                kw["bias"] = bias
            if scale is not None:
                kw["scale"] = scale
            return s.activation(out, in_, func, **kw)
        return self.em.s(fn, mark=mark)

    def _emit_all(self):
        em = self.em
        if self.arena_fill:
            if self.arena_fill is True:
                lo, hi = self.persist_mark, ARENA_BYTES
            else:
                lo, hi = self.arena_fill
            base = (lo + 63) // 64 * 64
            nf = (hi - base) // 4
            reg = self.ar.base[:, base:base + nf * 4].bitcast(F32)
            mf = self.vmemset(reg, 3.0e4, mark=True)
            for e in ("t", "s", "p", "q"):
                em.wait(e, "v", mf)
        em.dma("sq", self.X, self.d["x_fm"], sem="lx")
        em.dma("pq", self.identb, self.d["identf"], sem="lx")
        self.ld_x = em.dma("sq", self.identf, self.d["identf"], sem="lx")
        self.vmemset(self.ones1b[0:1, :], 1.0)
        self.vmemset(self.onescb, 1.0)
        self.vmemset(self.epsc[:, 0:1], EPS_RMS)
        self.vmemset(self.epsc[:, 1:2], EPS_LN)
        self.vmemset(self.epsc[:, 2:3], -1e30)

        self.m_ones = self.vmemset(self.ones1f[0:1, :], 1.0, mark=True)
        em.wait("t", "v", self.m_ones)
        em.wait("t", "lx", self.ld_x)

        self.prep_end = self.persist_mark
        if "ffn1" in self.phases or "ffn2" in self.phases:
            self._ffn_prep()
            em.full_barrier()
            self.prep_end = self.persist_mark
        if "attn_a" in self.phases:
            self._attn_a()
            self.em.full_barrier()
        if "ffn1" in self.phases:
            self._ffn_call()
            self.em.full_barrier()
        if "attn_b" in self.phases:
            self._attn_b()
            self.em.full_barrier()
        if "ffn2" in self.phases:
            self._ffn_call()
            self.em.full_barrier()
        self._writeback()

    def _writeback(self):
        em = self.em
        ar = self.ar
        ar.reset(self.persist_mark)
        otok = [ar.alloc(1024, F32) for _ in range(2)]
        em.full_barrier()

        prev_store = [0, 0]
        last = 0
        for tt in range(16):
            sl = tt % 2
            for dt in range(8):
                bank = self.pb[dt // 4 + 2 * sl]
                xsl = self.X[:, dt * TC + tt * 128: dt * TC + (tt + 1) * 128]
                self.tr(bank[:, (dt % 4) * 128:(dt % 4 + 1) * 128], xsl,
                        mark=(dt == 7), fp32=True)
            mt = em.counts["t"]
            em.wait("v", "t", mt)
            em.wait("v", f"g{sl}", prev_store[sl])
            mv = self.vcopy(otok[sl][:, 0:512], self.pb[2 * sl][:, 0:512])
            mv = self.vcopy(otok[sl][:, 512:1024],
                            self.pb[2 * sl + 1][:, 0:512], mark=True)
            em.wait("q", "v", mv)
            last = em.dma("sq", self.d["y_tok"][tt * 128:(tt + 1) * 128, :],
                          otok[sl], sem=f"g{sl}")
            prev_store[sl] = last
            em.wait("t", "v", mv)
        em.wait("q", "g0", em.dma_counts["g0"])
        em.wait("q", "g1", em.dma_counts["g1"])

    def _ffn_prep(self):
        em = self.em
        d = self.d
        ar = self.ar
        ar.reset(self.persist_mark)
        stage = [ar.alloc(4096, BF16) for _ in range(2)]
        nwp = ar.alloc(8, F32)
        l_nwp = em.dma("pq", nwp, d["normw"], sem="ld")
        em.wait("v", "ld", l_nwp)
        done_store = [0, 0]
        jobs = []
        for fp in range(16):
            jobs.append((d["w1p"][fp].transpose([1, 0, 2]),
                         d["w1bf"].ap()[fp].transpose([1, 0, 2]), 2048, 8))
            jobs.append((d["w3p"][fp].transpose([1, 0, 2]),
                         d["w3bf"].ap()[fp].transpose([1, 0, 2]), 2048, 8))
        for dt in range(8):
            jobs.append((d["w2t"][dt].transpose([1, 0, 2]),
                         d["w2bf"].ap()[dt].transpose([1, 0, 2]), 4096, 32))
        for i, (srcap, dst, cols, na) in enumerate(jobs):
            sl = i % 2
            em.wait("p", f"g{sl}", done_store[sl])
            st3 = stage[sl][:, 0:cols].rearrange("p (a c) -> p a c", a=na)
            ldv = em.dma("pq", st3, srcap, sem=f"e{sl}")
            if na == 8:
                em.wait("v", f"e{sl}", ldv)
                em.wait("v", f"g{sl}", done_store[sl])
                cpa = cols // 8
                for a_ in range(8):
                    mnw = self.vtsap(
                        stage[sl][:, a_ * cpa:(a_ + 1) * cpa],
                        stage[sl][:, a_ * cpa:(a_ + 1) * cpa],
                        nwp[:, a_:a_ + 1], ALU.mult, mark=(a_ == 7))
                em.wait("q", "v", mnw)
            else:
                em.wait("q", f"e{sl}", ldv)
            done_store[sl] = em.dma("sq", dst, st3, sem=f"g{sl}")
        self.prep_w_done = (em.dma_counts["g0"], em.dma_counts["g1"])

        bpos = ar.alloc(FF, BF16)
        bneg = ar.alloc(FF, BF16)
        chA = ar.alloc(FF, BF16)
        chB = ar.alloc(FF, BF16)
        braw = stage[0]
        chain_stores = []
        prev_sq = 0
        for mat, bT_in in enumerate([d["b1T"], d["b3T"]]):
            em.wait("p", "g0", done_store[0])
            em.wait("p", "sq", prev_sq)
            lb = em.dma("pq", braw[:, 0:FF], bT_in, sem="e0")
            em.wait("v", "e0", lb)
            em.wait("v", "sq", prev_sq)
            self.vts(bpos, braw[:, 0:FF], SCALING, ALU.mult)
            mneg = self.vts(bneg, braw[:, 0:FF], -SCALING, ALU.mult,
                            mark=True)
            z1 = self.vmemset(chA, 0.0)
            z2 = self.vmemset(chB, 0.0, mark=True)
            em.wait("q", "v", z2)
            placement = [(chA, 0, 0), (chA, 32, 3), (chA, 64, 5),
                         (chA, 96, 7), (chB, 0, 1), (chB, 32, 4),
                         (chB, 64, 6), (chB, 96, 2)]
            stv = 0
            for (ch, base, e) in placement:
                if e == 0:
                    stv = em.dma("sq", ch[0:16, :], bpos[0:16, :])
                else:
                    stv = em.dma("sq", ch[base:base + 16, :],
                                 bneg[16 * (e - 1):16 * e, :])
                    stv = em.dma("sq", ch[base + 16:base + 32, :],
                                 bpos[16 * e:16 * (e + 1), :])
            em.wait("q", "sq", stv)
            chd = d["chd"].ap()
            sA = em.dma("sq", chd[:, 2 * mat + 0].transpose([1, 0, 2]),
                        chA.rearrange("p (f c) -> p f c", f=16))
            sB = em.dma("sq", chd[:, 2 * mat + 1].transpose([1, 0, 2]),
                        chB.rearrange("p (f c) -> p f c", f=16))
            chain_stores.append(sB)
            prev_sq = sB
        self.prep_chain_done = chain_stores[-1]
        self.prep_end = ar.mark()

    def _ffn_call(self):
        em = self.em
        d = self.d
        ar = self.ar
        pb = self.pb
        pbb = self.pbb
        ar.reset(self.persist_mark)
        em.full_barrier()
        a = ar.alloc
        NCH, CH = 4, 512

        nxT = a(8 * CH, BF16)
        cact = a(32 * CH, BF16)
        chbuf = [a(4 * 256, BF16) for _ in range(2)]
        w13 = [a(2 * 8 * 256, BF16) for _ in range(2)]
        w2b = [a(32 * 128, BF16) for _ in range(2)]
        a2T = a(32 * 128, BF16)
        a1T, a3T = a(8 * 128, BF16), a(8 * 128, BF16)
        b2E, b2O = a(D, BF16), a(D, BF16)
        gateT = a(8 * 8, F32)
        xwf = a(CH, F32)
        rstdt = a(4, F32)
        mtok = a(4, F32)
        urow = a(8, F32)
        ubc = a(8, F32)
        nwb = a(8, BF16)
        normw = a(8, F32)
        selE, selO = a(128, BF16), a(128, BF16)
        bsel = a(8 * 128, BF16)
        r1A, r1W = a(CH, BF16), a(CH, BF16)
        r3A, r3W = a(CH, BF16), a(CH, BF16)
        xb = [a(CH, BF16), a(CH, BF16)]
        xsq = [a(CH, BF16), a(CH, BF16)]
        gtok = a(CH, F32)
        g32, m1, gm, m2 = a(32, F32), a(4, F32), a(32, F32), a(4, F32)
        dlt, eex, den = a(4, F32), a(4, F32), a(4, F32)
        w1r, w2r = a(4, F32), a(4, F32)
        is1, is2, wAx, wgf = a(32, F32), a(32, F32), a(32, F32), a(32, F32)
        wgt = a(32, BF16)
        wsum, wnr = a(4, F32), a(4, F32)
        wfT = a(CH, BF16)
        wsf = a(CH, BF16)
        bw = a(8 * CH, BF16)
        wslE, wslO = a(CH, BF16), a(CH, BF16)
        wr2E, wr2O = a(CH, BF16), a(CH, BF16)
        rows = a(5 * CH, F32)
        scr = a(CH, F32)
        zrow = a(8, BF16)
        sactb = [a(CH, BF16) for _ in range(2)]
        pactb = [a(CH, BF16) for _ in range(2)]
        wactb = [a(CH, BF16) for _ in range(2)]

        em.wait("q", "sq", self.prep_chain_done)
        em.dma("pq", a1T.rearrange("p (a c) -> p a c", a=8),
               d["a1T"].transpose([1, 0, 2]), sem="ld")
        em.dma("pq", a3T.rearrange("p (a c) -> p a c", a=8),
               d["a3T"].transpose([1, 0, 2]), sem="ld")
        em.dma("pq", a2T.rearrange("p (a c) -> p a c", a=32),
               d["a2T"].transpose([1, 0, 2]), sem="ld")
        em.dma("pq", b2E, d["b2SE"], sem="ld")
        em.dma("pq", b2O, d["b2SO"], sem="ld")
        em.dma("sq", gateT.rearrange("p (a c) -> p a c", a=8),
               d["gateT"].transpose([1, 0, 2]), sem="ld")
        em.dma("pq", normw, d["normw"], sem="ld")
        em.dma("pq", selE[0:8, 0:128], d["selE"], sem="ld")
        em.dma("pq", selO[0:8, 0:128], d["selO"], sem="ld")
        l_all = em.dma("pq", bsel[0:8, :], d["bsel"], sem="ld")
        l_a1 = l_a3 = l_a2 = l_b2e = l_gate = l_nw = l_sE = l_bsel = l_all

        dma_w13 = [0, 0]
        w13_free_t = [0, 0]
        dma_w2 = [0, 0]
        w2_free_v = [0, 0]
        silu_done = [0, 0]
        v1_done = [0, 0]
        util_free_v = 0
        rw_dma = 0
        unit = 0

        def prefetch_w13(tc, fp):
            slot = fp % 2
            em.wait("q", "t", w13_free_t[slot])
            v1 = em.dma("sq",
                        w13[slot][:, 0:8 * 256].rearrange(
                            "p (a c) -> p a c", a=8),
                        d["w1bf"].ap()[fp].transpose([1, 0, 2]),
                        sem=f"e{slot}")
            em.dma("sq",
                   w13[slot][:, 8 * 256:16 * 256].rearrange(
                       "p (a c) -> p a c", a=8),
                   d["w3bf"].ap()[fp].transpose([1, 0, 2]), sem=f"e{slot}")
            v3 = em.dma("sq",
                        chbuf[slot].rearrange("p (a c) -> p a c", a=4),
                        d["chd"].ap()[fp].transpose([1, 0, 2]),
                        sem=f"e{slot}")
            dma_w13[slot] = v3

        def prefetch_w2(tc, dt):
            slot = dt % 2
            em.wait("q", "v", w2_free_v[slot])
            dma_w2[slot] = em.dma(
                "sq", w2b[slot].rearrange("p (a c) -> p a c", a=32),
                d["w2bf"].ap()[dt].transpose([1, 0, 2]), sem=f"g{slot}")

        em.wait("q", "g0", self.prep_w_done[0])
        em.wait("q", "g1", self.prep_w_done[1])
        prefetch_w13(0, 0)
        prefetch_w13(0, 1)

        em.wait("s", "lx", self.ld_x)
        em.wait("v", "lx", self.ld_x)
        em.wait("t", "lx", self.ld_x)
        em.wait("s", "ld", em.dma_counts["ld"])
        em.wait("v", "ld", em.dma_counts["ld"])
        em.wait("t", "ld", em.dma_counts["ld"])
        mz = self.vmemset(zrow[0:1, :], 0.0, mark=True)
        em.wait("t", "v", mz)
        for kt in range(8):
            self.vtsap(a1T[:, kt * 128:(kt + 1) * 128],
                       a1T[:, kt * 128:(kt + 1) * 128],
                       normw[:, kt:kt + 1], ALU.mult)
            mfold = self.vtsap(a3T[:, kt * 128:(kt + 1) * 128],
                               a3T[:, kt * 128:(kt + 1) * 128],
                               normw[:, kt:kt + 1], ALU.mult, mark=(kt == 7))
        em.wait("t", "v", mfold)
        for kt in range(8):
            mu_ = self.mm(pb[6][0:1, 0:8], normw[:, kt:kt + 1],
                          gateT[:, kt * 8:(kt + 1) * 8], start=(kt == 0),
                          stop=(kt == 7), mark=(kt == 7))
        em.wait("v", "t", mu_)
        muv = self.vcopy(urow[0:1, 0:8], pb[6][0:1, 0:8], mark=True)
        em.wait("t", "v", muv)
        mub = self.mm(pb[7][:, 0:8], self.ones1f[0:1, :], urow[0:1, 0:8],
                      True, True, mark=True)
        em.wait("v", "t", mub)
        mubc = self.vcopy(ubc, pb[7][:, 0:8], mark=True)
        em.wait("t", "v", mubc)
        lnbufs = (xb, xsq, rows, scr)
        for tc in range(NCH):
            t0 = tc * CH
            X = self.X
            em.wait("t", "v", util_free_v)
            m_nx = self._attn_ln(t0, CH, nxT, CH, 0, lnbufs)
            util_free_v = m_nx
            if self.ser & 1:
                em.full_barrier()
            if tc == 1:
                self.dbg("nx_c1", nxT, 8 * CH, {"v": m_nx})
                self.dbg("srow_c1", rows[0:1, 3 * CH:4 * CH], CH, {"v": m_nx})

            em.wait("t", "ld", l_gate)
            em.wait("t", "v", m_nx)
            for kt in range(8):
                xsl = X[:, kt * TC + t0: kt * TC + t0 + CH]
                vw = self.vtsap(xwf, xsl, normw[:, kt:kt + 1], ALU.mult,
                                mark=True)
                em.wait("t", "v", vw)
                self.mm(pb[6][0:8, 0:CH], gateT[:, kt * 8:(kt + 1) * 8],
                        xwf, start=(kt == 0), stop=(kt == 7), mark=True)
                if kt < 7:
                    em.wait("v", "t", em.counts["t"])
            mg = em.counts["t"]
            em.wait("v", "t", mg)
            gv = self.vcopy(gtok[0:8, 0:CH], pb[6][0:8, 0:CH], mark=True)
            em.wait("t", "v", gv)
            for qq in range(4):
                self.tr(pb[7][:, qq * 8:(qq + 1) * 8],
                        gtok[0:8, qq * 128:(qq + 1) * 128], mark=(qq == 3),
                        fp32=True)
            mtr = em.counts["t"]
            em.wait("v", "t", mtr)
            for qq in range(4):
                self.tr(pb[6][:, 40 + qq:41 + qq],
                        rows[0:1, 3 * CH + qq * 128:3 * CH + (qq + 1) * 128],
                        fp32=True)
                self.tr(pb[6][:, 44 + qq:45 + qq],
                        rows[0:1, qq * 128:(qq + 1) * 128], mark=(qq == 3),
                        fp32=True)
            mrs = em.counts["t"]
            em.wait("v", "t", mrs)
            em.careful = True
            self.vcopy(rstdt, pb[6][:, 40:44])
            self.vcopy(mtok, pb[6][:, 44:48])
            self.vcopy(g32, pb[7][:, 0:32], mark=True)
            for aa in range(4):
                self.vtsap(wAx[:, 8 * aa:8 * aa + 8], ubc[:, 0:8],
                           mtok[:, aa:aa + 1], ALU.mult)
            self.vtt(g32, g32, wAx, ALU.subtract)
            for aa in range(4):
                self.vtsap(g32[:, 8 * aa:8 * aa + 8], g32[:, 8 * aa:8 * aa + 8],
                           rstdt[:, aa:aa + 1], ALU.mult)
            for aa in range(4):
                self.tree8(m1[:, aa:aa + 1], g32[:, 8 * aa:8 * aa + 8],
                           gm[:, 8 * aa:8 * aa + 8], ALU.max)
            for aa in range(4):
                self.vtsap(is1[:, 8 * aa:8 * aa + 8],
                           g32[:, 8 * aa:8 * aa + 8], m1[:, aa:aa + 1],
                           ALU.is_ge)
            self.vstt(gm, is1, self.epsc[:, 2:3], g32, ALU.mult, ALU.add)
            for aa in range(4):
                self.tree8(m2[:, aa:aa + 1], gm[:, 8 * aa:8 * aa + 8],
                           wAx[:, 8 * aa:8 * aa + 8], ALU.max)
            for aa in range(4):
                self.vtsap(is2[:, 8 * aa:8 * aa + 8],
                           gm[:, 8 * aa:8 * aa + 8], m2[:, aa:aa + 1],
                           ALU.is_ge)
            vd0 = self.vtt(dlt, m2, m1, ALU.subtract, mark=True)
            vd = vd0
            em.wait("s", "v", vd)
            self.sact(w1r, dlt, AF.Sigmoid, scale=-1.0)
            ee = self.sact(w2r, dlt, AF.Sigmoid, mark=True)
            em.wait("v", "s", ee)
            for aa in range(4):
                self.vtsap(wAx[:, 8 * aa:8 * aa + 8],
                           is1[:, 8 * aa:8 * aa + 8], w1r[:, aa:aa + 1],
                           ALU.mult)
                self.vtsap(wgf[:, 8 * aa:8 * aa + 8],
                           is2[:, 8 * aa:8 * aa + 8], w2r[:, aa:aa + 1],
                           ALU.mult)
            self.vtt(wgf, wgf, wAx, ALU.add)
            for aa in range(4):
                self.tree8(wsum[:, aa:aa + 1], wgf[:, 8 * aa:8 * aa + 8],
                           is1[:, 8 * aa:8 * aa + 8], ALU.add,
                           mark=(aa == 3))
            em.wait("s", "v", em.counts["v"])
            self.sact(wnr, wsum, AF.Ln)
            mwnr = self.sact(wnr, wnr, AF.Exp, scale=-1.0, mark=True)
            em.wait("v", "s", mwnr)
            wg_m = 0
            for aa in range(4):
                wg_m = self.vtsap(wgt[:, 8 * aa:8 * aa + 8],
                                  wgf[:, 8 * aa:8 * aa + 8],
                                  wnr[:, aa:aa + 1], ALU.mult,
                                  mark=(aa == 3))
            em.careful = False
            if tc == 1:
                self.dbg("nx_c1", nxT, 8 * CH, {"v": m_nx})
                self.dbg("srow_c1", rows[0:1, 3 * CH:4 * CH], CH, {"v": m_nx})
                self.dbg("g32_0", g32, 32, {"v": wg_m})
            em.wait("t", "v", wg_m)
            if tc == 1:
                self.dbg("wgt_c1", wgt, 32, {"v": wg_m})
                self.dbg("g32_c1", g32, 32, {"v": wg_m})
            for qq in range(4):
                self.tr(pbb[7][0:8, 64 + qq * 128: 64 + (qq + 1) * 128],
                        wgt[:, qq * 8:(qq + 1) * 8], mark=(qq == 3))
            mtw = em.counts["t"]
            em.wait("v", "t", mtw)
            wfv = self.vcopy(wfT[0:8, 0:CH], pbb[7][0:8, 64:64 + CH],
                             mark=True)
            wsv = self.vts(wsf[0:8, 0:CH], wfT[0:8, 0:CH], SCALING, ALU.mult,
                           mark=True)
            em.wait("t", "v", wfv)
            em.wait("t", "ld", l_bsel)
            bw_ev = {}
            for e in range(8):
                bank = pb[6] if e % 2 == 0 else pb[7]
                if e >= 2:
                    em.wait("t", "v", bw_ev[e - 2])
                self.mm(bank[:, 0:CH], bsel[0:8, e * 128:(e + 1) * 128],
                        wfT[0:8, 0:CH], True, True, mark=True)
                mk = em.counts["t"]
                em.wait("v", "t", mk)
                bw_ev[e] = self.vcopy(bw[:, e * CH:(e + 1) * CH],
                                      bank[:, 0:CH], mark=True)
            em.wait("t", "ld", l_sE)
            em.wait("t", "v", bw_ev[6])
            self.mm(pb[6][:, 0:CH], selE[0:8, 0:128], wsf[0:8, 0:CH],
                    True, True, mark=True)
            me_ = em.counts["t"]
            em.wait("t", "v", bw_ev[7])
            self.mm(pb[7][:, 0:CH], selO[0:8, 0:128], wsf[0:8, 0:CH],
                    True, True, mark=True)
            mo_ = em.counts["t"]
            em.wait("v", "t", me_)
            self.vcopy(wslE, pb[6][:, 0:CH])
            em.wait("v", "t", mo_)
            mwsl = self.vcopy(wslO, pb[7][:, 0:CH], mark=True)

            if self.ser & 2:
                em.full_barrier()
            em.wait("t", "ld", l_a1)
            em.wait("t", "v", mwsl)
            for kt in range(8):
                self.mm(pb[6][:, 0:CH], a1T[:, kt * 128:(kt + 1) * 128],
                        nxT[:, kt * CH:(kt + 1) * CH], start=(kt == 0),
                        stop=(kt == 7), mark=(kt == 7))
            mr1 = em.counts["t"]
            em.wait("t", "ld", l_a3)
            for kt in range(8):
                self.mm(pb[7][:, 0:CH], a3T[:, kt * 128:(kt + 1) * 128],
                        nxT[:, kt * CH:(kt + 1) * CH], start=(kt == 0),
                        stop=(kt == 7), mark=(kt == 7))
            mr3 = em.counts["t"]
            em.wait("v", "t", mr1)
            e1 = self.vcopy(r1A, pb[6][:, 0:CH], mark=True)
            em.wait("v", "t", mr3)
            e3 = self.vcopy(r3A, pb[7][:, 0:CH], mark=True)
            util_free_v = e3
            em.wait("q", "v", e1)
            em.dma("sq", r1W[32:96, :], r1A[48:112, :], sem="wd")
            em.dma("sq", r1W[96:128, :], r1A[16:48, :], sem="wd")
            em.wait("q", "v", e3)
            em.dma("sq", r3W[32:96, :], r3A[48:112, :], sem="wd")
            rw_dma = em.dma("sq", r3W[96:128, :], r3A[16:48, :], sem="wd")
            if tc == 1:
                self.dbg("nx_c1", nxT, 8 * CH, {"v": m_nx})
                self.dbg("srow_c1", rows[0:1, 3 * CH:4 * CH], CH, {"v": m_nx})
                pass

            def win(rA, rW):
                return {0: (rA, 0), 1: (rA, 0), 2: (rW, 96), 3: (rA, 32),
                        4: (rW, 32), 5: (rA, 64), 6: (rW, 64), 7: (rA, 96)}
            w1map = win(r1A, r1W)
            w3map = win(r3A, r3W)
            lhs1 = {0: (0, 0), 1: (1, 0), 2: (1, 96), 3: (0, 32),
                    4: (1, 32), 5: (0, 64), 6: (1, 64), 7: (0, 96)}
            lhs3 = {0: (2, 0), 1: (3, 0), 2: (3, 96), 3: (2, 32),
                    4: (3, 32), 5: (2, 64), 6: (3, 64), 7: (2, 96)}

            if self.ser & 4:
                em.full_barrier()
            em.wait("t", "ld", l_a2)
            em.wait("t", "wd", rw_dma)
            self.mm(pb[4][:, 0:CH], zrow[0:1, 0:1].broadcast_to([1, 128]),
                    zrow[0:1, 0:1].broadcast_to([1, CH]), True, False)
            self.mm(pb[5][:, 0:CH], zrow[0:1, 0:1].broadcast_to([1, 128]),
                    zrow[0:1, 0:1].broadcast_to([1, CH]), True, False)
            r2_started = set()
            for fp in range(16):
                slot = fp % 2
                for sub in range(2):
                    s_ = unit % 2
                    H1, H3 = pb[2 * s_], pb[2 * s_ + 1]
                    em.wait("t", f"e{slot}", dma_w13[slot])
                    if self.debug_dump and tc == 0 and fp == 0 and sub == 0:
                        pass
                    em.wait("t", "s", silu_done[s_])
                    em.wait("t", "v", v1_done[s_])
                    for half in range(2):
                        for kt in range(8):
                            self.mm(H1[:, half * 256:(half + 1) * 256],
                                    w13[slot][:, (0 * 8 + kt) * 256 +
                                              half * 128:
                                              (0 * 8 + kt) * 256 +
                                              half * 128 + 128],
                                    nxT[:, kt * CH + sub * 256:
                                        kt * CH + sub * 256 + 256],
                                    start=(kt == 0 and half == 0),
                                    stop=(kt == 7))
                    for half in range(2):
                        for kt in range(8):
                            self.mm(H3[:, half * 256:(half + 1) * 256],
                                    w13[slot][:, (1 * 8 + kt) * 256 +
                                              half * 128:
                                              (1 * 8 + kt) * 256 +
                                              half * 128 + 128],
                                    nxT[:, kt * CH + sub * 256:
                                        kt * CH + sub * 256 + 256],
                                    start=(kt == 0 and half == 0),
                                    stop=(kt == 7))
                    pend_r2 = None

                    def emit_r2(args, half_):
                        v1m, e_ = args
                        em.wait("t", "v", v1m)
                        bankR = pb[4] if e_ % 2 == 0 else pb[5]
                        rowb = 16 * e_ if e_ % 2 == 0 else 16 * (e_ - 1)
                        ft = 2 * fp + half_
                        self.mm(bankR[rowb:rowb + 16,
                                      sub * 256:sub * 256 + 256],
                                a2T[:, ft * 128 + 16 * e_:
                                    ft * 128 + 16 * e_ + 16],
                                pactb[s_][:, half_ * 256:
                                          (half_ + 1) * 256],
                                start=False, stop=False,
                                tp=(0, rowb) if rowb in (0, 32, 64)
                                else (0, 96))

                    for e in range(8):
                        (rT, rb) = w1map[e]
                        (ci, lb) = lhs1[e]
                        tp = (96, 0) if rb == 96 else None
                        if e > 0:
                            em.wait("t", "s", silu_done[s_])
                        for half in range(2):
                            self.mm(H1[:, half * 256:(half + 1) * 256],
                                    chbuf[slot][lb:lb + 32,
                                                ci * 256 + half * 128:
                                                ci * 256 + half * 128 + 128],
                                    rT[rb:rb + 32, sub * 256:sub * 256 + 256],
                                    start=False, stop=(half == 1),
                                    mark=(half == 1), tp=tp)
                        mh1 = em.counts["t"]
                        if pend_r2 is not None:
                            emit_r2(pend_r2, 0)
                        (rT3, rb3) = w3map[e]
                        (ci3, lb3) = lhs3[e]
                        tp3 = (96, 0) if rb3 == 96 else None
                        if e > 0:
                            em.wait("t", "v", v1_done[s_])
                        for half in range(2):
                            self.mm(H3[:, half * 256:(half + 1) * 256],
                                    chbuf[slot][lb3:lb3 + 32,
                                                ci3 * 256 + half * 128:
                                                ci3 * 256 + half * 128 + 128],
                                    rT3[rb3:rb3 + 32,
                                        sub * 256:sub * 256 + 256],
                                    start=False, stop=(half == 1),
                                    mark=(half == 1), tp=tp3)
                        mh3 = em.counts["t"]
                        if pend_r2 is not None:
                            emit_r2(pend_r2, 1)
                        em.wait("s", "t", mh1)
                        em.wait("s", "v", v1_done[s_])
                        silu_done[s_] = self.sact(sactb[s_], H1[:, 0:512],
                                                  AF.Silu, mark=True)
                        em.wait("v", "s", silu_done[s_])
                        em.wait("v", "t", mh3)
                        v1_done[s_] = self.vtt(pactb[s_], sactb[s_],
                                               H3[:, 0:512], ALU.mult,
                                               mark=True)
                        pend_r2 = (v1_done[s_], e)
                        bwe = bw[:, e * CH + sub * 256: e * CH +
                                 sub * 256 + 256]
                        self.vtt(wactb[s_][:, 0:256], pactb[s_][:, 0:256],
                                 bwe, ALU.mult)
                        self.vtt(wactb[s_][:, 256:512], pactb[s_][:, 256:512],
                                 bwe, ALU.mult)
                        co = cact.rearrange("p (f t) -> p f t", f=32)[
                            :, 2 * fp:2 * fp + 2, sub * 256:sub * 256 + 256]
                        wv = wactb[s_].rearrange("p (h t) -> p h t", h=2)
                        if e == 0:
                            self.vcopy(co, wv, mark=False)
                        else:
                            self.vtt(co, wv, co, ALU.add, mark=False)
                        unitlast_v = self.em.counts["v"]
                    emit_r2(pend_r2, 0)
                    emit_r2(pend_r2, 1)
                    self.em.t(lambda t: t.nop(), mark=True)
                    w13_free_t[slot] = em.counts["t"]
                    unit += 1
                if self.ser & 8:
                    em.full_barrier()
                if fp + 2 < 16:
                    prefetch_w13(tc, fp + 2)
                elif tc + 1 < NCH:
                    prefetch_w13(tc + 1, fp + 2 - 16)

            self.em.t(lambda t: t.nop(), mark=True)
            mr2 = em.counts["t"]
            if tc == 1:
                self.dbg("cact_c1", cact[:, 0:4096], 4096,
                         {"v": em.counts["v"], "t": mr2})
                em.wait("v", "pq", self.em.dma_counts["pq"])
            em.wait("v", "t", mr2)
            ve = self.vtt(wr2E, pb[4][:, 0:CH], wslE, ALU.mult, mark=True)
            vo = self.vtt(wr2O, pb[5][:, 0:CH], wslO, ALU.mult, mark=True)
            if tc == 1:
                self.dbg("wr2E_c1", wr2E, CH, {"v": ve})
                self.dbg("wr2O_c1", wr2O, CH, {"v": vo})
                self.dbg("wslE_c1", wslE, CH, {"v": ve})
            if self.ser & 16:
                em.full_barrier()
            down_resid = {}
            prefetch_w2(tc, 0)
            prefetch_w2(tc, 1)
            for dt in range(8):
                slot = dt % 2
                bank = pb[6] if dt % 2 == 0 else pb[7]
                em.wait("t", f"g{slot}", dma_w2[slot])
                em.wait("t", "v", util_free_v)
                if dt >= 2:
                    em.wait("t", "v", down_resid[dt - 2])
                for kt in range(32):
                    self.mm(bank[:, 0:CH],
                            w2b[slot][:, kt * 128:(kt + 1) * 128],
                            cact[:, kt * CH:(kt + 1) * CH],
                            start=(kt == 0), stop=False)
                em.wait("t", "v", ve if dt % 2 == 0 else vo)
                em.wait("t", "ld", l_b2e)
                self.mm(bank[:, 0:CH], b2E[:, dt * 128:(dt + 1) * 128],
                        wr2E, start=False, stop=False)
                md = self.mm(bank[:, 0:CH], b2O[:, dt * 128:(dt + 1) * 128],
                             wr2O, start=False, stop=True, mark=True)
                em.wait("v", "t", md)
                xsl = self.X[:, dt * TC + t0: dt * TC + t0 + CH]
                down_resid[dt] = self.vtt(xsl, xsl, bank[:, 0:CH], ALU.add,
                                          mark=True)
                if tc == 1 and dt in (0, 1):
                    self.dbg(f"Xc1after{dt}", xsl, CH, {"v": down_resid[dt]})
                w2_free_v[slot] = down_resid[dt]
                if dt + 2 < 8:
                    prefetch_w2(tc, dt + 2)
            util_free_v = down_resid[7]
            if self.ser & 32:
                em.full_barrier()
            em.wait("v", "t", em.counts["t"])
            em.wait("t", "v", vo)

    def _attn_ln(self, t0, CH, dst, dst_stride, dst_off, bufs):
        em = self.em
        pb = self.pb
        xb, xsq, rows, scr = bufs
        X = self.X
        em.wait("t", "v", em.counts["v"])
        em.wait("s", "lx", self.ld_x)
        st_m = {}
        for dt in range(8):
            sl = dt % 2
            xsl = X[:, dt * TC + t0: dt * TC + t0 + CH]
            if dt >= 2:
                em.wait("s", "t", st_m[dt - 2])
            self.sact(xb[sl], xsl, AF.Copy)
            m = self.sact(xsq[sl], xsl, AF.Square, mark=True)
            em.wait("t", "s", m)
            self.mm(pb[6][0:1, 0:CH], self.onescb, xb[sl],
                    start=(dt == 0), stop=(dt == 7))
            st_m[dt] = self.mm(pb[7][0:1, 0:CH], self.onescb, xsq[sl],
                               start=(dt == 0), stop=(dt == 7), mark=True)
        mstat = em.counts["t"]
        em.wait("v", "t", mstat)
        em.careful = True
        mean = rows[0:1, 0:CH]
        m2r = rows[0:1, CH:2 * CH]
        varr = rows[0:1, 2 * CH:3 * CH]
        srow = rows[0:1, 3 * CH:4 * CH]
        msr = rows[0:1, 4 * CH:5 * CH]
        self.vts(mean, pb[6][0:1, 0:CH], 1.0 / D, ALU.mult)
        self.vts(m2r, pb[7][0:1, 0:CH], 1.0 / D, ALU.mult)
        self.vtt(varr, mean, mean, ALU.mult)
        self.vtt(varr, m2r, varr, ALU.subtract)
        vv = self.vts(varr, varr, EPS_LN, ALU.add, mark=True)
        em.wait("s", "v", vv)
        self.sact(varr, varr, AF.Ln)
        rs = self.sact(srow, varr, AF.Exp, scale=-0.5, mark=True)
        em.wait("v", "s", rs)
        mm_ = self.vtt(msr, mean, srow, ALU.mult, mark=True)
        em.careful = False
        em.wait("t", "v", mm_)
        self.mm(pb[6][:, 0:CH], self.ones1f[0:1, :], srow, True, True)
        mb = self.mm(pb[7][:, 0:CH], self.ones1f[0:1, :], msr, True, True,
                     mark=True)
        em.wait("v", "t", mb)
        for dt in range(8):
            xsl = X[:, dt * TC + t0: dt * TC + t0 + CH]
            dsl = dst[:, dt * dst_stride + dst_off:
                      dt * dst_stride + dst_off + CH]
            self.vtt(scr, xsl, pb[6][:, 0:CH], ALU.mult)
            self.vtt(dsl, scr, pb[7][:, 0:CH], ALU.subtract, mark=(dt == 7))
        return em.counts["v"]

    def _rope_apply(self, x1ps, x2ps, tab, tcol, o1, o2, CH, H):
        cos = tab[0:H, tcol:tcol + CH]
        sin = tab[32:32 + H, tcol:tcol + CH]
        scr1 = self._rp_scr1[0:H, 0:CH]
        scr2 = self._rp_scr2[0:H, 0:CH]
        self.vtt(scr1, x1ps, cos, ALU.mult)
        self.vtt(scr2, x2ps, sin, ALU.mult)
        self.vtt(o1, scr1, scr2, ALU.subtract)
        self.vtt(scr1, x2ps, cos, ALU.mult)
        self.vtt(scr2, x1ps, sin, ALU.mult)
        return self.vtt(o2, scr1, scr2, ALU.add, mark=True)

    def _attn_a(self):
        em = self.em
        d = self.d
        ar = self.ar
        pb = self.pb
        ar.reset(self.prep_end)
        a = ar.alloc
        CH = 512

        nxT = a(8 * CH, BF16)
        ckv_own = a(TC, BF16)
        kr1 = a(TC, BF16)
        kr2 = a(TC, BF16)
        negrow = a(TC, BF16)
        ckvG = a(4 * TC, BF16)
        kaug = a(4 * TC, BF16)
        vh = a(64 * 65, BF16)
        attn_all = a(8 * CH, BF16)
        maskA = a(16 * CH, BF16)
        tabQ = a(TC, BF16)
        tabK = a(TC, BF16)
        wqsl = [a(8 * 128, BF16) for _ in range(2)]
        wkupsl = [a(128, BF16) for _ in range(2)]
        for _sl in range(2):
            self.vmemset(wqsl[_sl][64:96, :], 0.0)
            self.vmemset(wkupsl[_sl][64:96, :], 0.0)
        wosl = [a(8 * 128, BF16) for _ in range(2)]
        wkvd = a(8 * 128, BF16)
        wvup = a(1024, BF16)
        wkrope = a(32, BF16)
        wqrope = a(8 * 32, BF16)
        wtl = a(8 * 128, BF16)
        qt = a(CH, BF16)
        qaug = a(CH, BF16)
        q1s = a(CH, BF16)
        q2s = a(CH, BF16)
        expb = [a(CH, BF16) for _ in range(2)]
        bcsb = a(CH, BF16)
        linb = a(CH, BF16)
        xb = [a(CH, BF16), a(CH, BF16)]
        xsq = [a(CH, BF16), a(CH, BF16)]
        rows = a(5 * CH, F32)
        scr = a(CH, F32)
        self._rp_scr1 = a(CH, F32)
        self._rp_scr2 = a(CH, F32)
        nrope = rows[0:1, 0:CH]
        b2row = rows[0:1, CH:2 * CH]
        linv = rows[0:1, 2 * CH:3 * CH]
        mkrow = a(8, F32)

        em.dma("pq", wkvd.rearrange("p (a c) -> p a c", a=8),
               d["wkvd"].transpose([1, 0, 2]), sem="ld")
        em.dma("pq", wvup, d["wvup"], sem="ld")
        em.dma("pq", wkrope, d["wkrope"], sem="ld")
        em.dma("pq", wqrope.rearrange("p (a c) -> p a c", a=8),
               d["wqrope"].transpose([1, 0, 2]), sem="ld")
        em.dma("pq", maskA, d["maskA"], sem="ld")
        em.dma("pq", tabQ[0:16, :], d["cosAq"], sem="ld")
        em.dma("pq", tabQ[32:48, :], d["sinAq"], sem="ld")
        em.dma("pq", tabK[0:16, :], d["cosAk"], sem="ld")
        l_all = em.dma("pq", tabK[32:48, :], d["sinAk"], sem="ld")
        l_wkvd = l_wvup = l_wkrope = l_wqrope = l_mask = l_tq2 = l_tk2 \
            = l_all
        mneg = self.vmemset(negrow[0:1, :], -1.0, mark=True)
        self.vmemset(kaug[32:64, :], 0.0)
        self.vmemset(kaug[64:96, :], 0.0)
        self.vmemset(qaug[32:64, 0:CH], 0.0)
        m_zk = self.vmemset(qaug[64:96, 0:CH], 0.0, mark=True)
        em.wait("t", "v", m_zk)
        mvones = self.vmemset(
            vh.rearrange("p (kt x) -> p kt x", kt=64)[:, :, 64:65], 1.0,
            mark=True)
        lnbufs = (xb, xsq, rows, scr)

        for c in range(4):
            t0 = c * CH
            m_nx = self._attn_ln(t0, CH, nxT, CH, 0, lnbufs)
            em.wait("t", "v", m_nx)
            em.wait("t", "ld", l_wkvd)
            for kt in range(8):
                self.mm(pb[0][:, 0:CH], wkvd[:, kt * 128:(kt + 1) * 128],
                        nxT[:, kt * CH:(kt + 1) * CH], start=(kt == 0),
                        stop=(kt == 7), mark=(kt == 7))
            mck = em.counts["t"]
            em.wait("v", "t", mck)
            mce = self.vcopy(ckv_own[:, t0:t0 + CH], pb[0][:, 0:CH],
                             mark=True)
            em.wait("t", "v", mce)
            em.wait("t", "ld", l_wkrope)
            self.mm(pb[1][0:16, 0:CH], wkrope[:, 0:16],
                    ckv_own[:, t0:t0 + CH], True, True)
            mkr = self.mm(pb[2][0:16, 0:CH], wkrope[:, 16:32],
                          ckv_own[:, t0:t0 + CH], True, True, mark=True)
            em.wait("v", "t", mkr)
            em.wait("v", "ld", l_tk2)
            self._rope_apply(pb[1][0:16, 0:CH], pb[2][0:16, 0:CH], tabK, t0,
                             kr1[0:16, t0:t0 + CH], kr2[0:16, t0:t0 + CH],
                             CH, 16)
        mkrope_all = em.counts["v"]
        em.wait("q", "v", mkrope_all)
        em.wait("q", "v", mneg)
        cc1 = d["cc1_in"].ap()
        em.dma("sq", cc1[0:128, :], ckv_own, sem="cs")
        em.dma("sq", cc1[128:144, :], kr1[0:16, :], sem="cs")
        em.dma("sq", cc1[144:160, :], kr2[0:16, :], sem="cs")
        ck_dma = em.dma("sq", cc1[160:161, :], negrow[0:1, :], sem="cs")
        em.wait("p", "cs", ck_dma)

        def ccfn(g):
            return g.collective_compute(
                "AllGather", ALU.bypass,
                replica_groups=[[0, 1, 2, 3], [4, 5, 6, 7]],
                ins=[d["cc1_in"].ap().opt()], outs=[d["cc1_out"].ap().opt()])
        em.collective(ccfn, "cc1")
        em.wait("p", "cc1", 1)
        em.wait("p", "v", m_zk)
        cc1o = d["cc1_out"].ap()
        for r in range(4):
            em.dma("pq", ckvG[:, r * TC:(r + 1) * TC],
                   cc1o[161 * r:161 * r + 128, :], sem="ga")
            l_gath = em.dma("pq", kaug[0:33, r * TC:(r + 1) * TC],
                            cc1o[161 * r + 128:161 * r + 161, :], sem="ga")

        em.wait("s", "ga", l_gath)
        em.wait("t", "ga", l_gath)
        for r in range(4):
            em.wait("t", "v", em.counts["v"])
            mk_m = 0
            for cc_ in range(4):
                sl = cc_ % 2
                if cc_ >= 2:
                    em.wait("s", "t", mk_m)
                self.sact(xsq[sl], ckvG[:, r * TC + cc_ * CH:
                                        r * TC + (cc_ + 1) * CH], AF.Square)
                m1_ = self.sact(xb[sl][0:33, :],
                                kaug[0:33, r * TC + cc_ * CH:
                                     r * TC + (cc_ + 1) * CH],
                                AF.Square, mark=True)
                em.wait("t", "s", m1_)
                self.mm(pb[3][0:1, 0:CH], self.onescb, xsq[sl],
                        start=(cc_ == 0), stop=False)
                mk_m = self.mm(pb[3][0:1, 0:CH], self.onescb[0:33, :],
                               xb[sl][0:33, :], start=False, stop=(cc_ == 3),
                               mark=True)
            em.wait("v", "t", mk_m)
            em.careful = True
            self.vred(rows[0:1, 0:1], pb[3][0:1, 0:CH], ALU.max)
            self.vcopy(mkrow[0:1, r:r + 1], rows[0:1, 0:1], mark=True)
            em.careful = False
        em.careful = True
        self.vred(mkrow[0:1, 4:5], mkrow[0:1, 0:4], ALU.max, mark=True)
        mks = em.counts["v"]
        em.wait("s", "v", mks)
        self.sact(mkrow[0:1, 5:6], mkrow[0:1, 4:5], AF.Ln)
        mkf = self.sact(mkrow[0:1, 6:7], mkrow[0:1, 5:6], AF.Exp, scale=0.5,
                        mark=True)
        em.careful = False

        wq_dma = [0, 0]
        wq_free = [0, 0]
        wo_dma = [0, 0]
        wo_free = [0, 0]

        def prefetch_wq(h):
            slot = h % 2
            em.wait("p", "t", wq_free[slot])
            em.dma("pq", wqsl[slot][0:64, :].rearrange(
                "p (a c) -> p a c", a=8), d["wqTt"][h].transpose([1, 0, 2]),
                sem=f"e{slot}")
            wq_dma[slot] = em.dma("pq", wkupsl[slot][0:64, :],
                                  d["wkupT"][h], sem=f"e{slot}")

        exp_free = [0, 0]
        sc_free = [0, 0]

        for c in range(4):
            t0 = c * CH
            m_nx = self._attn_ln(t0, CH, nxT, CH, 0, lnbufs)
            em.wait("t", "v", m_nx)
            em.wait("t", "ld", l_wqrope)
            for kt in range(8):
                self.mm(pb[4][0:16, 0:CH], wqrope[:, kt * 32:kt * 32 + 16],
                        nxT[:, kt * CH:(kt + 1) * CH], start=(kt == 0),
                        stop=(kt == 7))
                self.mm(pb[5][0:16, 0:CH],
                        wqrope[:, kt * 32 + 16:kt * 32 + 32],
                        nxT[:, kt * CH:(kt + 1) * CH], start=(kt == 0),
                        stop=(kt == 7), mark=(kt == 7))
            mqr = em.counts["t"]
            em.wait("v", "t", mqr)
            em.wait("v", "ld", l_tq2)
            mrope = self._rope_apply(pb[4][0:16, 0:CH], pb[5][0:16, 0:CH],
                                     tabQ, t0, q1s[0:16, 0:CH],
                                     q2s[0:16, 0:CH], CH, 16)
            mq1 = self.vcopy(qaug[0:16, 0:CH], q1s[0:16, 0:CH], mark=True)
            em.wait("q", "v", mq1)
            qa_dma = em.dma("sq", qaug[16:32, 0:CH], q2s[0:16, 0:CH], sem="qa")
            em.wait("s", "v", mq1)
            s1 = self.sact(xsq[0][0:16, 0:CH], q1s[0:16, 0:CH], AF.Square)
            s2 = self.sact(xsq[1][0:16, 0:CH], q2s[0:16, 0:CH], AF.Square,
                           mark=True)
            em.wait("t", "s", s2)
            em.wait("t", "v", mq1)
            self.mm(pb[4][0:1, 0:CH], self.onescb[0:16, :],
                    xsq[0][0:16, 0:CH], True, False)
            mnr = self.mm(pb[4][0:1, 0:CH], self.onescb[0:16, :],
                          xsq[1][0:16, 0:CH], False, True, mark=True)
            em.wait("v", "t", mnr)
            mnrv = self.vcopy(nrope[0:1, 0:CH], pb[4][0:1, 0:CH], mark=True)

            prefetch_wq(0)
            prefetch_wq(1)
            for h in range(NH):
                slot = h % 2
                em.wait("t", f"e{slot}", wq_dma[slot])
                em.wait("t", "v", mnrv)
                for dt in range(8):
                    bank = pb[4] if dt < 4 else pb[5]
                    self.mm(bank[:, (dt % 4) * 128:(dt % 4 + 1) * 128],
                            wqsl[slot][0:96, dt * 128:(dt + 1) * 128],
                            wkupsl[slot][0:96, :], True, True,
                            mark=(dt == 3 or dt == 7))
                mwt = em.counts["t"]
                em.wait("v", "t", mwt)
                self.vcopy(wtl[:, 0:512], pb[4][:, 0:CH])
                mwv = self.vcopy(wtl[:, 512:1024], pb[5][:, 0:CH], mark=True)
                wq_free[slot] = mwt
                em.wait("t", "v", mwv)
                em.wait("t", "s", getattr(self, "_mqs_prev", 0))
                for dt in range(8):
                    self.mm(pb[6][:, 0:CH], wtl[:, dt * 128:(dt + 1) * 128],
                            nxT[:, dt * CH:(dt + 1) * CH],
                            start=(dt == 0), stop=(dt == 7), mark=(dt == 7))
                mqt = em.counts["t"]
                em.wait("s", "t", mqt)
                mqs = self.em.s(lambda s, o=qt, i=pb[6][:, 0:CH]:
                                s.mul(o, i, 1.0 / SCALE_A), mark=True)
                self._mqs_prev = mqs
                sqq = self.sact(xsq[0], qt, AF.Square, mark=True)
                em.wait("t", "s", sqq)
                em.wait("t", "v", getattr(self, "_b2_prev", 0))
                self.mm(pb[7][0:1, 0:CH], self.onescb, xsq[0], True, True,
                        mark=True)
                mbn = em.counts["t"]
                em.wait("v", "t", mbn)
                em.wait("v", "v", mnrv)
                em.careful = True
                self._b2_prev = self.vtt(b2row[0:1, 0:CH], pb[7][0:1, 0:CH],
                                         nrope[0:1, 0:CH], ALU.add,
                                         mark=True)
                em.wait("s", "v", em.counts["v"])
                self.sact(b2row[0:1, 0:CH], b2row[0:1, 0:CH], AF.Ln)
                mbs = self.sact(b2row[0:1, 0:CH], b2row[0:1, 0:CH], AF.Exp,
                                scale=0.5, mark=True)
                em.wait("v", "s", mbs)
                em.wait("v", "s", mkf)
                self.vtsap(b2row[0:1, 0:CH], b2row[0:1, 0:CH],
                           mkrow[0:1, 6:7], ALU.mult)
                self.vts(b2row[0:1, 0:CH], b2row[0:1, 0:CH], 1.02, ALU.mult)
                em.wait("v", "qa", qa_dma)
                mba = self.vcopy(qaug[32:33, 0:CH], b2row[0:1, 0:CH],
                                 mark=True)
                em.careful = False
                em.wait("t", "v", mba)
                em.wait("t", "ga", l_gath)
                em.wait("t", "ld", l_wvup)
                em.wait("t", "v", mvones)
                nkt = 16 * (c + 1)
                for grp in range(nkt // 8):
                    for ii in range(8):
                        idx = grp * 8 + ii
                        r_ = idx % 4
                        m_ = idx // 4
                        self.mm(pb[3][:, ii * 64:(ii + 1) * 64],
                                ckvG[:, r_ * TC + 128 * m_:
                                     r_ * TC + 128 * m_ + 128],
                                wvup[:, h * 64:(h + 1) * 64],
                                start=True, stop=True, mark=(ii == 7))
                    mvb = em.counts["t"]
                    em.wait("v", "t", mvb)
                    mvc = self.em.v(
                        lambda v, grp=grp:
                        v.tensor_copy(
                            vh.rearrange("p (kt x) -> p kt x", kt=64)
                            [:, grp * 8:(grp + 1) * 8, 0:64],
                            pb[3][:, 0:512].rearrange(
                                "p (i x) -> p i x", i=8)), mark=True)
                    em.wait("t", "v", mvc)
                em.wait("t", "s", mqs)
                em.wait("t", "ld", l_mask)
                if self.debug_dump and c == 0 and h == 0:
                    self.dbg("a_qt", qt, 512, {"s": mqs})
                    self.dbg("a_qaug", qaug[0:33, :], 512, {"v": mba})
                    self.dbg("a_wtl", wtl, 1024, {"v": mwv})
                    self.dbg("a_ckvG", ckvG[:, 0:512], 512, {"ga": l_gath})
                    self.dbg("a_kaug", kaug[0:33, 0:512], 512,
                             {"ga": l_gath})
                    self.dbg("a_vh", vh[:, 0:520], 520, {"v": em.counts["v"]})
                pend = None
                for kt in range(nkt):
                    r_ = kt % 4
                    m_ = kt // 4
                    sb = pb[0] if kt % 2 == 0 else pb[1]
                    ebuf = expb[kt % 2]
                    em.wait("t", "s", sc_free[kt % 2])
                    self.mm(sb[:, 0:CH],
                            ckvG[:, r_ * TC + 128 * m_:
                                 r_ * TC + 128 * m_ + 128],
                            qt, True, False)
                    msc = self.mm(sb[:, 0:CH],
                                  kaug[0:96, r_ * TC + 128 * m_:
                                       r_ * TC + 128 * m_ + 128],
                                  qaug[0:96, 0:CH], False, True, mark=True)
                    if pend is not None:
                        peb, psrc, pval, pvt, pkt = pend
                        em.wait("t", psrc, pval)
                        exp_free[pkt % 2] = self.mm(
                            pb[2][0:65, 0:CH], pvt, peb,
                            start=(pkt == 0), stop=False, mark=True)
                    em.wait("s", "t", msc)
                    em.wait("s", "t", exp_free[kt % 2])
                    mex = self.sact(ebuf, sb[:, 0:CH], AF.Exp, mark=True)
                    sc_free[kt % 2] = mex
                    if m_ >= 4 * c:
                        i_ = m_ - 4 * c
                        em.wait("v", "s", mex)
                        mmk = self.vtt(ebuf, ebuf,
                                       maskA[:, (4 * i_ + r_) * CH:
                                             (4 * i_ + r_ + 1) * CH],
                                       ALU.mult, mark=True)
                        wsrc, wval = "v", mmk
                    else:
                        wsrc, wval = "s", mex
                    if (self.debug_dump and c == 0 and h == 0
                            and kt in (0, 5)):
                        self.dbg(f"a_exp{kt}", ebuf, 512,
                                 {wsrc: wval})
                    idx = (m_ % 4) * 4 + r_
                    cb = m_ // 4
                    vtile = vh[:, (16 * cb + idx) * 65:
                               (16 * cb + idx) * 65 + 65]
                    pend = (ebuf, wsrc, wval, vtile, kt)
                peb, psrc, pval, pvt, pkt = pend
                em.wait("t", psrc, pval)
                exp_free[pkt % 2] = self.mm(
                    pb[2][0:65, 0:CH], pvt, peb,
                    start=(pkt == 0), stop=True, mark=True)
                mavd = em.counts["t"]
                em.wait("s", "t", mavd)
                self.sact(linv[0:1, 0:CH], pb[2][64:65, 0:CH], AF.Ln)
                mlb = self.sact(linb[0:1, 0:CH], linv[0:1, 0:CH], AF.Exp,
                                scale=-1.0, mark=True)
                em.wait("t", "s", mlb)
                mbc = self.mm(pb[3][0:64, 0:CH], self.ones1b[0:1, 0:64],
                              linb[0:1, 0:CH], True, True, mark=True)
                em.wait("v", "t", mbc)
                self.vcopy(bcsb[0:64, 0:CH], pb[3][0:64, 0:CH], mark=True)
                dst = attn_all[:, (h // 2) * CH:(h // 2 + 1) * CH]
                dsth = dst[(h % 2) * 64:(h % 2) * 64 + 64, :]
                mat = self.vtt(dsth, pb[2][0:64, 0:CH], bcsb[0:64, 0:CH],
                               ALU.mult, mark=True)
                em.wait("t", "v", mat)
                if self.debug_dump and c == 0 and h == 0:
                    self.dbg("a_att0", dsth, 512, {"v": mat})
                    self.dbg("a_linv", linv[0:1, :], 512, {"v": mat})
                if h + 2 < NH:
                    prefetch_wq(h + 2)
            wo_res = {}
            for mt in range(8):
                slot = mt % 2
                em.wait("p", "t", wo_free[slot])
                wo_dma[slot] = em.dma(
                    "pq", wosl[slot].rearrange("p (a c) -> p a c", a=8),
                    d["woT"][:, mt].transpose([1, 0, 2]), sem=f"g{slot}")
                em.wait("t", f"g{slot}", wo_dma[slot])
                bank = pb[6] if mt % 2 == 0 else pb[7]
                if mt >= 2:
                    em.wait("t", "v", wo_res[mt - 2])
                for kt2 in range(8):
                    self.mm(bank[:, 0:CH],
                            wosl[slot][:, kt2 * 128:(kt2 + 1) * 128],
                            attn_all[:, kt2 * CH:(kt2 + 1) * CH],
                            start=(kt2 == 0), stop=(kt2 == 7),
                            mark=(kt2 == 7))
                mwo = em.counts["t"]
                wo_free[slot] = mwo
                em.wait("v", "t", mwo)
                xsl = self.X[:, mt * TC + t0: mt * TC + t0 + CH]
                wo_res[mt] = self.vtt(xsl, xsl, bank[:, 0:CH], ALU.add,
                                      mark=True)
            em.wait("t", "v", em.counts["v"])
            em.wait("s", "v", em.counts["v"])

    def _attn_b(self):
        em = self.em
        d = self.d
        ar = self.ar
        pb = self.pb
        ar.reset(self.prep_end)
        a = ar.alloc
        CH = 512

        nx2 = a(8 * TC, BF16)
        vh = a(64 * 65, BF16)
        kaugB = a(4 * TC, BF16)
        vtb = [a(TC, BF16) for _ in range(2)]
        attn_allB = a(4 * 8 * CH, BF16)
        tabQ = a(TC, BF16)
        tabK = a(TC, BF16)
        wqh = [a(8 * 64, BF16) for _ in range(2)]
        wkh = [a(8 * 64, BF16) for _ in range(2)]
        wvh = [a(8 * 64, BF16) for _ in range(2)]
        wosl = [a(8 * 128, BF16) for _ in range(2)]
        bout = a(8, F32)
        qaug = a(CH, BF16)
        ksb = a(CH, BF16)
        vsb = a(CH, BF16)
        expb = [a(CH, BF16) for _ in range(2)]
        bcsb = a(CH, BF16)
        linb = a(CH, BF16)
        xb = [a(CH, BF16), a(CH, BF16)]
        xsq = [a(CH, BF16), a(CH, BF16)]
        rows = a(5 * CH, F32)
        scr = a(CH, F32)
        self._rp_scr1 = a(CH, F32)
        self._rp_scr2 = a(CH, F32)
        b2row = rows[0:1, 0:CH]
        linv = rows[0:1, CH:2 * CH]
        mkrow = a(8, F32)
        onesneg = expb[0]
        vones = expb[1]

        em.dma("pq", tabQ[0:32, :], d["cosBq"], sem="ld")
        em.dma("pq", tabQ[32:64, :], d["sinBq"], sem="ld")
        em.dma("pq", tabK[0:32, :], d["cosBk"], sem="ld")
        em.dma("pq", tabK[32:64, :], d["sinBk"], sem="ld")
        l_bout = em.dma("pq", bout, d["bout"], sem="ld")
        l_tq2 = l_tk2 = l_bout
        lnbufs = (xb, xsq, rows, scr)
        cc2 = d["cc2_in"].ap()
        self.vmemset(kaugB[64:96, :], 0.0)
        self.vmemset(
            vh.rearrange("p (kt x) -> p kt x", kt=64)[:, :, 64:65], 1.0)
        m_zkb = self.vmemset(qaug[64:96, 0:CH], 0.0, mark=True)
        em.wait("t", "v", m_zkb)
        mno = self.vmemset(onesneg[0:16, :], -1.0, mark=True)
        mvo = self.vmemset(vones[0:16, :], 1.0, mark=True)
        em.wait("q", "v", mvo)
        for cq in range(4):
            em.dma("sq", cc2[64:1040:65, cq * CH:(cq + 1) * CH],
                   onesneg[0:16, :], sem="cs")
            em.dma("sq", cc2[1040 + 64:2080:65, cq * CH:(cq + 1) * CH],
                   vones[0:16, :], sem="cs")

        kv_dma = [0, 0]
        kv_free = [0, 0]

        def prefetch_kv(h):
            slot = h % 2
            em.wait("p", "t", kv_free[slot])
            em.dma("pq", wkh[slot].rearrange("p (a c) -> p a c", a=8),
                   d["wkh"][h].transpose([1, 0, 2]), sem=f"e{slot}")
            kv_dma[slot] = em.dma(
                "pq", wvh[slot].rearrange("p (a c) -> p a c", a=8),
                d["wvh"][h].transpose([1, 0, 2]), sem=f"e{slot}")

        for c in range(4):
            self._attn_ln(c * CH, CH, nx2, TC, c * CH, lnbufs)
        kv_st = 0
        prefetch_kv(0)
        prefetch_kv(1)
        for c in range(4):
            t0 = c * CH
            for h in range(NH):
                slot = h % 2
                em.wait("t", f"e{slot}", kv_dma[slot])
                em.wait("t", "v", getattr(self, "_kv_prev", 0))
                for kt in range(8):
                    self.mm(pb[0][0:64, 0:CH],
                            wkh[slot][:, kt * 64:(kt + 1) * 64],
                            nx2[:, kt * TC + t0:kt * TC + t0 + CH],
                            start=(kt == 0), stop=(kt == 7), mark=(kt == 7))
                mk_ = em.counts["t"]
                for kt in range(8):
                    self.mm(pb[1][0:64, 0:CH],
                            wvh[slot][:, kt * 64:(kt + 1) * 64],
                            nx2[:, kt * TC + t0:kt * TC + t0 + CH],
                            start=(kt == 0), stop=(kt == 7), mark=(kt == 7))
                mv_ = em.counts["t"]
                kv_free[slot] = mv_
                em.wait("v", "t", mk_)
                em.wait("v", "ld", l_tk2)
                em.wait("v", "cs", kv_st)
                self._rope_apply(pb[0][0:32, 0:CH], pb[0][32:64, 0:CH],
                                 tabK, t0, ksb[0:32, 0:CH],
                                 ksb[32:64, 0:CH], CH, 32)
                em.wait("v", "t", mv_)
                mvs = self.vcopy(vsb[0:64, 0:CH], pb[1][0:64, 0:CH],
                                 mark=True)
                self._kv_prev = mvs
                em.wait("q", "v", mvs)
                if self.debug_dump and c == 0 and h == 0:
                    self.dbg("b_ksb", ksb[0:64, 0:CH], 512,
                             {"v": em.counts["v"]})
                    self.dbg("b_vsb", vsb[0:64, 0:CH], 512, {"v": mvs})
                em.dma("sq", cc2[65 * h:65 * h + 64, t0:t0 + CH],
                       ksb[0:64, 0:CH], sem="cs")
                kv_st = em.dma("sq", cc2[1040 + 65 * h:1040 + 65 * h + 64,
                                         t0:t0 + CH], vsb[0:64, 0:CH],
                               sem="cs")
                if h + 2 < NH:
                    prefetch_kv(h + 2)
            if c + 1 < 4:
                prefetch_kv(0)
                prefetch_kv(1)
        em.wait("p", "cs", kv_st)

        def ccfn(g):
            return g.collective_compute(
                "AllGather", ALU.bypass,
                replica_groups=[[0, 1, 2, 3], [4, 5, 6, 7]],
                ins=[d["cc2_in"].ap().opt()], outs=[d["cc2_out"].ap().opt()])
        em.collective(ccfn, "cc2")
        em.wait("p", "cc2", 1)
        cc2o = d["cc2_out"].ap()
        if self.ser & 64:
            return

        wq_dma2 = [0, 0]
        wq_free2 = [0, 0]

        def prefetch_wq(h):
            slot = h % 2
            em.wait("p", "t", wq_free2[slot])
            wq_dma2[slot] = em.dma(
                "pq", wqh[slot].rearrange("p (a c) -> p a c", a=8),
                d["wqh"][h].transpose([1, 0, 2]), sem=f"h{slot}")

        prefetch_wq(0)
        prefetch_wq(1)
        exp_free = [0, 0]
        sc_free = [0, 0]
        vt_dma = [0, 0]
        vt_free = [0, 0]
        ka_dma = 0
        nheads = 1 if (self.ser & 128) else NH
        for h in range(nheads):
            slot = h % 2
            prev_ka = ka_dma
            em.wait("p", "t", em.counts["t"] if h > 0 else 0)
            em.wait("p", "v", m_zkb)
            for r in range(4):
                ka_dma = em.dma("pq", kaugB[0:65, r * TC:(r + 1) * TC],
                                cc2o[2080 * r + 65 * h:
                                     2080 * r + 65 * h + 65, :], sem="ga")
            em.wait("s", "ga", ka_dma)
            for r in range(4):
                em.wait("t", "v", em.counts["v"])
                mk_m = 0
                for cc_ in range(4):
                    sl = cc_ % 2
                    if cc_ >= 2:
                        em.wait("s", "t", mk_m)
                    m1_ = self.sact(xsq[sl][0:65, :],
                                    kaugB[0:65, r * TC + cc_ * CH:
                                          r * TC + (cc_ + 1) * CH],
                                    AF.Square, mark=True)
                    em.wait("t", "s", m1_)
                    mk_m = self.mm(pb[3][0:1, 0:CH], self.onescb[0:65, :],
                                   xsq[sl][0:65, :], start=(cc_ == 0),
                                   stop=(cc_ == 3), mark=True)
                em.wait("v", "t", mk_m)
                em.careful = True
                self.vred(rows[0:1, 0:1], pb[3][0:1, 0:CH], ALU.max)
                self.vcopy(mkrow[0:1, r:r + 1], rows[0:1, 0:1], mark=True)
                em.careful = False
            em.careful = True
            self.vred(mkrow[0:1, 4:5], mkrow[0:1, 0:4], ALU.max, mark=True)
            mks = em.counts["v"]
            em.wait("s", "v", mks)
            self.sact(mkrow[0:1, 5:6], mkrow[0:1, 4:5], AF.Ln)
            mkf = self.sact(mkrow[0:1, 6:7], mkrow[0:1, 5:6], AF.Exp,
                            scale=0.5, mark=True)
            em.careful = False
            em.wait("t", "v", em.counts["v"])
            for r in range(4) if not (self.ser & 512) else []:
                vslot = r % 2
                em.wait("p", "t", vt_free[vslot])
                vt_dma[vslot] = em.dma(
                    "pq", vtb[vslot][0:65, :],
                    cc2o[2080 * r + 1040 + 65 * h:
                         2080 * r + 1040 + 65 * h + 65, :], sem=f"g{vslot}")
                em.wait("t", f"g{vslot}", vt_dma[vslot])
                for grp in range(2):
                    for ii in range(8):
                        mtile = grp * 8 + ii
                        self.tr(self.pbb[3][:, ii * 64:ii * 64 + 64],
                                vtb[vslot][0:64, 128 * mtile:
                                           128 * mtile + 128],
                                mark=(ii == 7))
                    mtr = em.counts["t"]
                    vt_free[vslot] = mtr
                    em.wait("v", "t", mtr)
                    mvc = self.em.v(
                        lambda v, r=r, grp=grp:
                        v.tensor_copy(
                            vh.rearrange("p (kt x) -> p kt x", kt=64)
                            [:, r * 16 + grp * 8:r * 16 + (grp + 1) * 8,
                             0:64],
                            self.pbb[3][:, 0:512].rearrange(
                                "p (i x) -> p i x", i=8)), mark=True)
                    em.wait("t", "v", mvc)
            em.wait("t", f"h{slot}", wq_dma2[slot])
            for c in range(4):
                t0 = c * CH
                em.wait("t", "v", getattr(self, "_qb_prev", 0))
                for kt in range(8):
                    self.mm(pb[4][0:64, 0:CH],
                            wqh[slot][:, kt * 64:(kt + 1) * 64],
                            nx2[:, kt * TC + t0:kt * TC + t0 + CH],
                            start=(kt == 0), stop=(kt == 7), mark=(kt == 7))
                mq_ = em.counts["t"]
                em.wait("v", "t", mq_)
                em.wait("v", "ld", l_tq2)
                mrope = self._rope_apply(pb[4][0:32, 0:CH],
                                         pb[4][32:64, 0:CH], tabQ, t0,
                                         qaug[0:32, 0:CH],
                                         qaug[32:64, 0:CH], CH, 32)
                self._qb_prev = mrope
                em.wait("s", "v", mrope)
                s1 = self.sact(xsq[0][0:64, :], qaug[0:64, 0:CH],
                               AF.Square, mark=True)
                em.wait("t", "s", s1)
                em.wait("t", "v", mrope)
                self.mm(pb[5][0:1, 0:CH], self.onescb[0:64, :],
                        xsq[0][0:64, :], True, True, mark=True)
                mbn = em.counts["t"]
                em.wait("v", "t", mbn)
                em.careful = True
                self._qb_prev = max(self._qb_prev, self.vcopy(
                    b2row[0:1, 0:CH], pb[5][0:1, 0:CH], mark=True))
                em.wait("s", "v", em.counts["v"])
                self.sact(b2row[0:1, 0:CH], b2row[0:1, 0:CH], AF.Ln)
                mbs = self.sact(b2row[0:1, 0:CH], b2row[0:1, 0:CH], AF.Exp,
                                scale=0.5, mark=True)
                em.wait("v", "s", mbs)
                em.wait("v", "s", mkf)
                self.vtsap(b2row[0:1, 0:CH], b2row[0:1, 0:CH],
                           mkrow[0:1, 6:7], ALU.mult)
                self.vts(b2row[0:1, 0:CH], b2row[0:1, 0:CH], 1.02, ALU.mult)
                mba = self.vcopy(qaug[64:65, 0:CH], b2row[0:1, 0:CH],
                                 mark=True)
                em.careful = False
                em.wait("t", "v", mba)
                if self.debug_dump and c == 0 and h == 0:
                    self.dbg("b_qaug", qaug[0:65, 0:CH], 512, {"v": mba})
                    self.dbg("b_kaug", kaugB[0:65, 0:512], 512,
                             {"ga": ka_dma})
                    self.dbg("b_vh", vh[:, 0:65], 65, {"v": em.counts["v"]})
                    self.dbg("b_nx2", nx2[:, 0:512], 512,
                             {"v": em.counts["v"]})
                pend = None
                for kt in range(64) if not (self.ser & 256) else []:
                    r_ = kt // 16
                    m_ = kt % 16
                    sb = pb[0] if kt % 2 == 0 else pb[1]
                    ebuf = expb[kt % 2]
                    em.wait("t", "s", sc_free[kt % 2])
                    msc = self.mm(sb[:, 0:CH],
                                  kaugB[0:96, r_ * TC + 128 * m_:
                                        r_ * TC + 128 * m_ + 128],
                                  qaug[0:96, 0:CH], True, True, mark=True)
                    if pend is not None:
                        peb, pval, pvt, pkt = pend
                        em.wait("t", "s", pval)
                        exp_free[pkt % 2] = self.mm(
                            pb[2][0:65, 0:CH], pvt, peb,
                            start=(pkt == 0), stop=False, mark=True)
                    em.wait("s", "t", msc)
                    em.wait("s", "t", exp_free[kt % 2])
                    mex = self.sact(ebuf, sb[:, 0:CH], AF.Exp, mark=True)
                    sc_free[kt % 2] = mex
                    if (self.debug_dump and c == 0 and h == 0
                            and kt in (0, 17)):
                        self.dbg(f"b_exp{kt}", ebuf, 512, {"s": mex})
                    pend = (ebuf, mex, vh[:, kt * 65:kt * 65 + 65], kt)
                if self.ser & 256:
                    continue
                peb, pval, pvt, pkt = pend
                em.wait("t", "s", pval)
                exp_free[pkt % 2] = self.mm(
                    pb[2][0:65, 0:CH], pvt, peb,
                    start=(pkt == 0), stop=True, mark=True)
                mavd = em.counts["t"]
                em.wait("s", "t", mavd)
                self.sact(linv[0:1, 0:CH], pb[2][64:65, 0:CH], AF.Ln)
                mlb = self.sact(linb[0:1, 0:CH], linv[0:1, 0:CH], AF.Exp,
                                scale=-1.0, mark=True)
                em.wait("t", "s", mlb)
                mbc = self.mm(pb[3][0:64, 0:CH], self.ones1b[0:1, 0:64],
                              linb[0:1, 0:CH], True, True, mark=True)
                em.wait("v", "t", mbc)
                self.vcopy(bcsb[0:64, 0:CH], pb[3][0:64, 0:CH], mark=True)
                dst = attn_allB[:, (c * 8 + h // 2) * CH:
                                (c * 8 + h // 2 + 1) * CH]
                dsth = dst[(h % 2) * 64:(h % 2) * 64 + 64, :]
                mat = self.vtt(dsth, pb[2][0:64, 0:CH], bcsb[0:64, 0:CH],
                               ALU.mult, mark=True)
                em.wait("t", "v", mat)
                if self.debug_dump and c == 0 and h == 0:
                    self.dbg("b_att0", dsth, 512, {"v": mat})
                    self.dbg("b_linv", linv[0:1, :], 512, {"v": mat})
            wq_free2[slot] = em.counts["t"]
            if h + 2 < nheads:
                prefetch_wq(h + 2)
        if self.ser & (128 | 256 | 512):
            return

        wo_dma = [0, 0]
        wo_free = [0, 0]
        em.wait("v", "ld", l_bout)
        em.wait("t", "v", em.counts["v"])
        wo_res = {}
        for c in range(4):
            t0 = c * CH
            for mt in range(8):
                slot = mt % 2
                em.wait("p", "t", wo_free[slot])
                wo_dma[slot] = em.dma(
                    "pq", wosl[slot].rearrange("p (a c) -> p a c", a=8),
                    d["woutT"][:, mt].transpose([1, 0, 2]), sem=f"e{slot}")
                em.wait("t", f"e{slot}", wo_dma[slot])
                bank = pb[6] if mt % 2 == 0 else pb[7]
                key = (c, mt)
                pk = (c, mt - 2) if mt >= 2 else (c - 1, 6 + mt)
                if pk in wo_res:
                    em.wait("t", "v", wo_res[pk])
                for kt2 in range(8):
                    self.mm(bank[:, 0:CH],
                            wosl[slot][:, kt2 * 128:(kt2 + 1) * 128],
                            attn_allB[:, (c * 8 + kt2) * CH:
                                      (c * 8 + kt2 + 1) * CH],
                            start=(kt2 == 0), stop=(kt2 == 7),
                            mark=(kt2 == 7))
                mwo = em.counts["t"]
                wo_free[slot] = mwo
                em.wait("v", "t", mwo)
                xsl = self.X[:, mt * TC + t0: mt * TC + t0 + CH]
                wo_res[key] = self.vstt(xsl, bank[:, 0:CH],
                                        bout[:, mt:mt + 1], xsl,
                                        ALU.add, ALU.add, mark=True)


def em_replay_one(em, ename, eobj, sems):
    for item in em.streams[ename]:
        if item[0] == "op":
            _, fn, sem_name, inc = item
            ins = fn(eobj)
            if sem_name is not None:
                if inc is None:
                    ins.then_inc(sems[sem_name])
                else:
                    ins.then_inc(sems[sem_name], inc)
        else:
            _, src, val = item
            eobj.wait_ge(sems[src], val)



def _f32(x):
    return np.ascontiguousarray(np.asarray(x), dtype=np.float32)


def host_ffn_inputs(inp):
    w1 = _f32(inp["L0_ffn_w1"])
    w3 = _f32(inp["L0_ffn_w3"])
    w2 = _f32(inp["L0_ffn_w2"])
    gate = _f32(inp["L0_ffn_gate_w"])
    a1 = _f32(inp["L0_ffn_a1"])
    b1 = _f32(inp["L0_ffn_b1"])
    a3 = _f32(inp["L0_ffn_a3"])
    b3 = _f32(inp["L0_ffn_b3"])
    a2 = _f32(inp["L0_ffn_a2"])
    b2 = _f32(inp["L0_ffn_b2"])
    nw = _f32(inp["L0_ffn_norm_w"])
    d = {}
    d["w1p"] = np.ascontiguousarray(
        w1.T.reshape(8, 128, 16, 256).transpose(2, 0, 1, 3))
    d["w3p"] = np.ascontiguousarray(
        w3.T.reshape(8, 128, 16, 256).transpose(2, 0, 1, 3))
    d["w2t"] = np.ascontiguousarray(
        w2.T.reshape(32, 128, 8, 128).transpose(2, 0, 1, 3))
    d["gateT"] = np.ascontiguousarray(gate.T.reshape(8, 128, 8))
    d["a1T"] = np.ascontiguousarray(
        a1.reshape(NE * RK, D).T.reshape(8, 128, 128))
    d["a3T"] = np.ascontiguousarray(
        a3.reshape(NE * RK, D).T.reshape(8, 128, 128))
    d["a2T"] = np.ascontiguousarray(
        a2.reshape(NE * RK, FF).T.reshape(32, 128, 128))
    d["b1T"] = np.ascontiguousarray(b1.transpose(0, 2, 1).reshape(NE * RK, FF))
    d["b3T"] = np.ascontiguousarray(b3.transpose(0, 2, 1).reshape(NE * RK, FF))
    b2s = b2.transpose(0, 2, 1).reshape(NE * RK, D)
    bE = np.zeros((128, D), np.float32)
    bO = np.zeros((128, D), np.float32)
    for e in range(NE):
        blk = b2s[16 * e:16 * (e + 1)]
        if e % 2 == 0:
            bE[16 * e:16 * e + 16] = blk
        else:
            bO[16 * (e - 1):16 * (e - 1) + 16] = blk
    d["b2SE"] = bE
    d["b2SO"] = bO
    d["normw"] = np.ascontiguousarray(nw.reshape(8, 128).T)
    sE = np.zeros((8, 128), np.float32)
    sO = np.zeros((8, 128), np.float32)
    for e in range(NE):
        if e % 2 == 0:
            sE[e, 16 * e:16 * e + 16] = 1.0
        else:
            sO[e, 16 * (e - 1):16 * (e - 1) + 16] = 1.0
    d["selE"] = sE
    d["selO"] = sO
    bsel = np.zeros((8, 8, 128), np.float32)
    for e in range(NE):
        bsel[e, e, :] = 1.0
    d["bsel"] = bsel.reshape(8 * 8, 128).reshape(8, 8 * 128) if False \
        else np.ascontiguousarray(bsel.transpose(1, 0, 2).reshape(8, 8 * 128))
    d["identf"] = np.eye(128, dtype=np.float32)
    return d


def host_attn_shared(inp):
    d = {}
    wkvd = _f32(inp["L0_attn_a_W_kv_down"])
    d["wkvd"] = np.ascontiguousarray(wkvd.reshape(8, 128, 128))
    wq = _f32(inp["L0_attn_a_W_q"])
    d["wqTt"] = np.ascontiguousarray(
        wq.T.reshape(16, 64, 8, 128).transpose(0, 2, 1, 3))
    d["wkupT"] = np.ascontiguousarray(
        _f32(inp["L0_attn_a_W_k_up"]).T.reshape(16, 64, 128))
    d["wvup"] = _f32(inp["L0_attn_a_W_v_up"])
    d["wqrope"] = np.ascontiguousarray(
        _f32(inp["L0_attn_a_W_q_rope"]).reshape(8, 128, 32))
    d["wkrope"] = _f32(inp["L0_attn_a_W_k_rope"])
    d["woT"] = np.ascontiguousarray(
        _f32(inp["L0_attn_a_W_o"]).reshape(8, 128, 8, 128).transpose(
            0, 2, 1, 3))
    wqkv = _f32(inp["L0_attn_b_w_qkv"])
    for nm, off in [("wqh", 0), ("wkh", 1024), ("wvh", 2048)]:
        d[nm] = np.ascontiguousarray(
            wqkv[:, off:off + 1024].reshape(8, 128, 16, 64).transpose(
                2, 0, 1, 3))
    d["woutT"] = np.ascontiguousarray(
        _f32(inp["L0_attn_b_w_out"]).reshape(8, 128, 8, 128).transpose(
            0, 2, 1, 3))
    d["bout"] = np.ascontiguousarray(
        _f32(inp["L0_attn_b_b_out"]).reshape(8, 128).T)
    return d


def host_attn_core(core):
    b, j = core // 4, core % 4
    pos = (4.0 * np.arange(TC) + j).astype(np.float64)
    d = {}
    invfA = (10000.0 ** (-np.arange(0, DR, 2) / DR)).astype(np.float64)
    angA = pos[None, :] * invfA[:, None]
    d["cosAq"] = (np.cos(angA) / SCALE_A).astype(np.float32)
    d["sinAq"] = (np.sin(angA) / SCALE_A).astype(np.float32)
    d["cosAk"] = np.cos(angA).astype(np.float32)
    d["sinAk"] = np.sin(angA).astype(np.float32)
    invfB = (10000.0 ** (-np.arange(32) / 32.0)).astype(np.float64)
    angB = pos[None, :] * invfB[:, None]
    d["cosBq"] = (np.cos(angB) * SCALE_B).astype(np.float32)
    d["sinBq"] = (np.sin(angB) * SCALE_B).astype(np.float32)
    d["cosBk"] = np.cos(angB).astype(np.float32)
    d["sinBk"] = np.sin(angB).astype(np.float32)
    mask = np.zeros((128, 16, 512), np.float32)
    p = np.arange(128)[:, None]
    f = np.arange(512)[None, :]
    for i in range(4):
        for r in range(4):
            mask[:, 4 * i + r, :] = (512 * i + 4 * p + r <= 4 * f + j)
    d["maskA"] = np.ascontiguousarray(mask.reshape(128, 16 * 512))
    return d


def host_shard_x(x, core):
    b, j = core // 4, core % 4
    xs = np.asarray(x, dtype=np.float32)[b, j::4, :]
    xf = np.ascontiguousarray(xs.T)
    return np.ascontiguousarray(
        xf.reshape(8, 128, TC).transpose(1, 0, 2).reshape(128, 8 * TC))


def host_unshard_y(shards):
    y = np.empty((B, T, D), dtype=np.float32)
    for c in range(NCORES):
        b, j = c // 4, c % 4
        y[b, j::4, :] = shards[c]
    return y


def kernel(**inputs):
    bld = Builder()
    nc = bld.build()
    shared = host_ffn_inputs(inputs)
    shared.update(host_attn_shared(inputs))
    in_maps = []
    for c in range(NCORES):
        m = dict(shared)
        m.update(host_attn_core(c))
        m["x_fm"] = host_shard_x(inputs["L0_attn_a_x"], c)
        in_maps.append(m)
    res = run_bass_kernel_spmd(nc, in_maps, core_ids=list(range(NCORES)))
    shards = [res.results[c]["y_tok"] for c in range(NCORES)]
    return host_unshard_y(shards)
